# version 02
# baseline (speedup 1.0000x reference)
import os
import numpy as np

import concourse.bass as bass
import concourse.mybir as mybir
from concourse.bass_utils import run_bass_kernel_spmd

F32 = mybir.dt.float32
F16 = mybir.dt.float16
AF = mybir.ActivationFunctionType
ALU = mybir.AluOpType
AX = mybir.AxisListType

D = 4096
I = 11008
NJ = D // 128
NI = I // 128
IC = 256
NCH = I // IC
NOCT = (NI + 7) // 8
KSEL = 5504.0
SUBW = 2048
KSUB = KSEL * SUBW / I
CSCALE = float(I) / SUBW
NPS = 7
NPF = 5
NPROBE = NPS + NPF
LN_EPS = 1e-5
SCALING = 2.0
ACENTER = 0.28125
SEMS = ["LDW", "LDT", "SEL", "MM", "EVV", "EVA", "ST", "STG", "RL", "HS"]


class Em:
    def __init__(self, nc, rec, final):
        self.nc, self.rec, self.final = nc, rec, final
        self.h, self.n = {}, {}
        for nm in SEMS:
            self.h[nm] = nc.semaphore(nm).__enter__()
            self.n[nm] = 0

    def inc(self, ins, sem, by=1):
        self.n[sem] += by
        ins.then_inc(self.h[sem], by)

    def mark(self, sem, key):
        if self.final:
            assert self.rec[key] == self.n[sem], (key, self.rec[key], self.n[sem])
        else:
            self.rec[key] = self.n[sem]

    def wait(self, eng, sem, key):
        if self.final:
            eng.wait_ge(self.h[sem], self.rec[key])

    def waitv(self, eng, sem, v):
        if self.final:
            eng.wait_ge(self.h[sem], v)


def _emit(n_tok, taps, rec, final, stages=4):
    NT = n_tok // 128
    NSLAB = n_tok // 512
    THALF = n_tok // 2
    NTH = THALF // 128
    assert NTH % 2 == 0 and NSLAB >= 1

    nc = bass.Bass("TRN2", target_bir_lowering=False)

    x_in = nc.dram_tensor("x", [n_tok, D], F32, kind="ExternalInput").ap()
    xT_in = nc.dram_tensor("xT", [D, n_tok], F32, kind="ExternalInput").ap()
    wgT = nc.dram_tensor("wgT", [D, I], F32, kind="ExternalInput").ap()
    wuT = nc.dram_tensor("wuT", [D, I], F32, kind="ExternalInput").ap()
    wdT = nc.dram_tensor("wdT", [I, D], F32, kind="ExternalInput").ap()
    wbT = nc.dram_tensor("wbT", [D, D], F32, kind="ExternalInput").ap()
    gwP = nc.dram_tensor("gwP", [128, NJ * 128], F32, kind="ExternalInput").ap()
    laT = nc.dram_tensor("laT", [D, 128], F32, kind="ExternalInput").ap()
    lbT = nc.dram_tensor("lbT", [128, D], F32, kind="ExternalInput").ap()
    bias_in = nc.dram_tensor("bias", [1, D], F32, kind="ExternalInput").ap()
    ident_in = nc.dram_tensor("ident", [128, 128], F32, kind="ExternalInput").ap()
    expm_in = nc.dram_tensor("expm", [8, 128], F32, kind="ExternalInput").ap()
    ones_in = nc.dram_tensor("ones", [128, 128], F32, kind="ExternalInput").ap()

    y_out = nc.dram_tensor("y", [n_tok, D], F32, kind="ExternalOutput").ap()

    skind = "ExternalOutput" if taps else "Internal"
    a_scr = nc.dram_tensor("a_scr", [n_tok, I], F16, kind=skind).ap()
    p0_scr = nc.dram_tensor("p0_scr", [n_tok, I], F16, kind=skind).ap()
    p_scr = nc.dram_tensor("p_scr", [n_tok, I], F16, kind=skind).ap()
    r_scr = nc.dram_tensor("r_scr", [n_tok, D], F32, kind=skind).ap()
    h_scr = nc.dram_tensor("h_scr", [n_tok, D], F32, kind=skind).ap()
    h_f16 = nc.dram_tensor("h_f16", [n_tok, D], F16, kind=skind).ap()
    y_scr = nc.dram_tensor("y_scr", [n_tok, D], F32, kind=skind).ap()
    wd_f16 = nc.dram_tensor("wd_f16", [I, D], F16, kind="Internal").ap()
    wb_f16 = nc.dram_tensor("wb_f16", [D, D], F16, kind="Internal").ap()
    bias_f16 = nc.dram_tensor("bias_f16", [1, D], F16, kind="Internal").ap()
    if taps:
        tsel = nc.dram_tensor("tsel", [n_tok, 96], F32, kind="ExternalOutput").ap()
        dbg4 = nc.dram_tensor("dbg4", [n_tok // 512, 128, 1152], F32,
                              kind="ExternalOutput").ap()
        tsel2 = nc.dram_tensor("tsel2", [n_tok, 96], F32,
                               kind="ExternalOutput").ap()

    SB_BASE = 20480

    def A(name, shape, dtype, offset):
        return nc.alloc_sbuf_tensor_at(name, shape, dtype, offset=offset + SB_BASE)

    ident32 = A("ident32", [128, 128], F32, offset=0).ap()
    state = A("state", [128, 256], F32, offset=512).ap()
    sml = A("sml", [128, 256], F32, offset=1536).ap()
    P0 = 2560
    xTb = A("xTb", [128, NJ * THALF], F16, offset=P0).ap()
    o = P0 + NJ * THALF * 2
    guB = A("guB", [128, 2 * NJ * 512], F16, offset=o).ap()
    o += 2 * NJ * 512 * 2
    sRing = A("sRing", [128, 2 * IC], F32, offset=o).ap()
    o += 2 * IC * 4
    aStrip = A("aStrip", [128, 2 * NTH * IC], F16, offset=o).ap()
    o += 2 * NTH * IC * 2
    p0Strip = A("p0Strip", [128, 2 * NTH * IC], F16, offset=o).ap()
    o += 2 * NTH * IC * 2
    xsc = A("xsc", [128, 2 * IC], F32, offset=o).ap()
    st1_end = o + 2 * IC * 4
    pTs = A("pTs", [128, NI * 512], F16, offset=P0).ap()
    o2 = P0 + NI * 512 * 2
    wdR = A("wdR", [128, 2 * 8 * 512], F16, offset=o2).ap()
    o2 += 2 * 8 * 512 * 2
    xCh = A("xCh", [128, 2 * 512], F32, offset=o2).ap(); o2 += 2 * 512 * 4
    rCh = A("rCh", [128, 2 * 512], F32, offset=o2).ap(); o2 += 2 * 512 * 4
    sqJ = A("sqJ", [128, 512], F16, offset=o2).ap(); o2 += 512 * 2
    rlCh = A("rlCh", [128, 2 * 512], F32, offset=o2).ap(); o2 += 2 * 512 * 4
    hCh = A("hCh", [128, 2 * 512], F32, offset=o2).ap(); o2 += 2 * 512 * 4
    wbB = A("wbB", [128, 2 * NJ * 512], F16, offset=P0).ap()
    o4 = P0 + 2 * NJ * 512 * 2
    hTb = A("hTb", [128, NJ * 512], F16, offset=o4).ap()
    o4 += NJ * 512 * 2
    assert o4 <= P0 + NI * 512 * 2 + 2 * 8 * 512 * 2, o4
    o4 = o2
    gwB = A("gwB", [128, NJ * 128], F16, offset=o4).ap(); o4 += NJ * 128 * 2
    laB = A("laB", [128, NJ * 128], F16, offset=o4).ap(); o4 += NJ * 128 * 2
    lbB = A("lbB", [128, D], F16, offset=o4).ap(); o4 += D * 2
    biasOC = A("biasOC", [128, 2 * 512], F16, offset=o4).ap(); o4 += 2 * 512 * 2
    expB = A("expB", [8, 128], F16, offset=o4).ap(); o4 += 128 * 2
    onesB = A("onesB", [128, 128], F16, offset=o4).ap(); o4 += 128 * 2
    glT = A("glT", [8, 512], F32, offset=o4).ap(); o4 += 512 * 4
    glog = A("glog", [128, 32], F32, offset=o4).ap(); o4 += 32 * 4
    wful = A("wful", [128, 32], F32, offset=o4).ap(); o4 += 32 * 4
    wfT = A("wfT", [8, 512], F16, offset=o4).ap(); o4 += 512 * 2
    lowS = A("lowS", [128, 512], F32, offset=o4).ap(); o4 += 512 * 4
    lsT = A("lsT", [128, 512], F16, offset=o4).ap(); o4 += 512 * 2
    hRes, yCh, ylCh, yoCh, sqJ4 = xCh, rCh, rlCh, hCh, sqJ
    SELOFF = max(st1_end, o4)
    aT = A("aT", [128, I], F16, offset=SELOFF).ap()
    wrk = A("wrk", [128, I], F16, offset=SELOFF + I * 2).ap()
    SEL_END = SELOFF + 2 * I * 2
    assert SEL_END <= 208767, SEL_END

    S_LOHI, S_SLOHI, S_TH, S_CNT, S_MV, S_T1, S_T2 = 0, 2, 4, 5, 6, 10, 12
    S_E, S_DEN, S_N2 = 14, 15, 16
    S_SUM, S_SQ = 18, 50
    S_MU, S_RS, S_BI = 82, 86, 90
    S_M1, S_M2, S_T3 = 94, 98, 102
    S_BTH, S_BERR, S_BT1, S_BT2 = 104, 105, 106, 107
    S_MU_O, S_RS_O, S_BI_O = 108, 112, 116
    S_MU_P = (82, 108)
    S_RS_P = (86, 112)
    S_BI_P = (90, 116)
    S_MU2_P = (120, 132)
    S_RS2_P = (124, 136)
    S_BI2_P = (128, 140)

    pk = [nc.alloc_psum_tensor(f"bank{i}", [128, 512], F32).ap() for i in range(8)]

    em = Em(nc, rec, final)

    with nc.Block() as block:
        def g_prelude(g):
            ins = g.dma_start(out=ident32[:], in_=ident_in)
            em.inc(ins, "LDW", 16)
            em.mark("LDW", "ident")

        block.gpsimd(g_prelude)

        def sel_ld_a(q, t):
            half_needed = 0 if (t + 1) * 128 <= THALF else 1
            em.wait(q, "ST", ("s1st", half_needed, NCH - 1))
            if t >= 1:
                em.wait(q, "EVV", ("selmask", t - 1))
            ins = q.dma_start(out=aT[:], in_=a_scr[t * 128:(t + 1) * 128, :])
            em.inc(ins, "SEL", 16)
            em.mark("SEL", ("sella", t))

        def sel_ld_p0(q, t):
            em.wait(q, "EVV", ("selprobed", t))
            if t >= 1:
                em.wait(q, "SEL", ("selstore", t - 1))
            ins = q.dma_start(out=wrk[:], in_=p0_scr[t * 128:(t + 1) * 128, :])
            em.inc(ins, "SEL", 16)
            em.mark("SEL", ("selp0", t))

        def sel_store(q, t):
            em.wait(q, "EVV", ("selmask", t))
            ins = q.dma_start(out=p_scr[t * 128:(t + 1) * 128, :], in_=wrk[:])
            em.inc(ins, "SEL", 16)
            em.mark("SEL", ("selstore", t))
            if taps:
                ins = q.dma_start(out=tsel[t * 128:(t + 1) * 128, :],
                                  in_=state[:, 160:256])
                em.inc(ins, "SEL", 16)
                ins = q.dma_start(out=tsel2[t * 128:(t + 1) * 128, :],
                                  in_=sml[:, 0:96])
                em.inc(ins, "SEL", 16)

        def sel_pieces_for_tile(t, sink):
            lohi = state[:, S_LOHI:S_LOHI + 2]
            slohi = state[:, S_SLOHI:S_SLOHI + 2]
            th = state[:, S_TH:S_TH + 1]
            cnt = state[:, S_CNT:S_CNT + 1]
            mvq = state[:, S_MV:S_MV + 4]
            t1 = state[:, S_T1:S_T1 + 2]
            t2 = state[:, S_T2:S_T2 + 2]
            e = state[:, S_E:S_E + 1]
            den = state[:, S_DEN:S_DEN + 1]
            n2 = state[:, S_N2:S_N2 + 1]

            def p_init(v, t=t):
                v.memset(lohi[:, 0:1], -1.0)
                v.memset(lohi[:, 1:2], 32.0)
                v.memset(slohi[:, 0:1], float(SUBW))
                v.memset(slohi[:, 1:2], 0.0)
                v.memset(th, 0.2 - ACENTER)
                v.memset(state[:, S_BTH:S_BTH + 1], 0.2 - ACENTER)
                v.memset(state[:, S_BERR:S_BERR + 1], 1e30)
                v.drain()

            sink.append(("init", t, p_init))

            for it in range(NPROBE):
                W = I if it >= NPS else SUBW
                K = KSEL if it >= NPS else KSUB

                def p_probe(v, t=t, it=it, W=W, K=K):
                    if it == 0:
                        em.wait(v, "SEL", ("sella", t))
                        if t >= 1:
                            em.wait(v, "SEL", ("selstore", t - 1))
                    v.tensor_scalar(wrk[:, 0:W], aT[:, 0:W], th, 0.0,
                                    ALU.is_ge, ALU.add, accum_out=cnt)
                    v.drain()
                    if it >= NPS:
                        bth = state[:, S_BTH:S_BTH + 1]
                        berr = state[:, S_BERR:S_BERR + 1]
                        bt1 = state[:, S_BT1:S_BT1 + 1]
                        bt2 = state[:, S_BT2:S_BT2 + 1]
                        v.tensor_single_scalar(bt1, cnt, K, ALU.subtract)
                        v.drain()
                        v.tensor_tensor(bt1, bt1, bt1, ALU.mult)
                        v.drain()
                        v.tensor_tensor(bt2, bt1, berr, ALU.is_lt)
                        v.drain()
                        v.tensor_tensor(berr, berr, bt1, ALU.min)
                        v.tensor_tensor(bt1, th, bth, ALU.subtract)
                        v.drain()
                        v.tensor_tensor(bt1, bt1, bt2, ALU.mult)
                        v.drain()
                        v.tensor_tensor(bth, bth, bt1, ALU.add)
                        v.drain()
                    v.tensor_single_scalar(mvq[:, 0:1], cnt, K, ALU.is_ge)
                    v.drain()
                    v.tensor_scalar(mvq[:, 1:2], mvq[:, 0:1], -1.0, 1.0,
                                    ALU.mult, ALU.add)
                    v.drain()
                    v.tensor_copy(mvq[:, 2:3], mvq[:, 1:2])
                    v.tensor_copy(mvq[:, 3:4], mvq[:, 0:1])
                    v.drain()
                    v.tensor_scalar(t1, lohi, th, None, ALU.subtract)
                    v.drain()
                    v.tensor_tensor(t2, mvq[:, 0:2], t1, ALU.mult)
                    v.drain()
                    v.tensor_tensor(lohi, lohi, t2, ALU.subtract)
                    v.tensor_scalar(t1, slohi, cnt, None, ALU.subtract)
                    v.drain()
                    v.tensor_tensor(t2, mvq[:, 0:2], t1, ALU.mult)
                    v.drain()
                    v.tensor_tensor(slohi, slohi, t2, ALU.subtract)
                    v.drain()
                    v.tensor_single_scalar(t1, slohi, K, ALU.subtract)
                    v.drain()
                    v.tensor_tensor(t1, t1, mvq[:, 2:4], ALU.mult)
                    v.drain()
                    v.tensor_single_scalar(t1, t1, 0.5, ALU.mult)
                    v.drain()
                    v.tensor_tensor(slohi, slohi, t1, ALU.subtract)
                    v.drain()
                    if it == NPS - 1:
                        v.tensor_single_scalar(lohi[:, 0:1], th, -0.008, ALU.add)
                        v.tensor_single_scalar(lohi[:, 1:2], th, 0.008, ALU.add)
                        v.memset(slohi[:, 0:1], KSEL + 300.0)
                        v.memset(slohi[:, 1:2], KSEL - 300.0)
                        v.drain()
                    if it == 0:
                        v.memset(th, 0.45 - ACENTER)
                        v.drain()
                    elif it < NPROBE - 1:
                        v.tensor_tensor(e, lohi[:, 1:2], lohi[:, 0:1], ALU.subtract)
                        v.tensor_tensor(den, slohi[:, 0:1], slohi[:, 1:2],
                                        ALU.subtract)
                        v.drain()
                        v.reciprocal(den, den)
                        v.drain()
                        v.tensor_tensor(e, e, den, ALU.mult)
                        v.drain()
                        kk = KSEL if it >= NPS - 1 else KSUB
                        v.tensor_single_scalar(n2, slohi[:, 0:1], kk, ALU.subtract)
                        v.drain()
                        v.tensor_tensor(e, e, n2, ALU.mult)
                        v.drain()
                        v.tensor_tensor(th, lohi[:, 0:1], e, ALU.add)
                        v.drain()
                    if it == NPROBE - 1:
                        ins = v.tensor_copy(state[:, S_T3:S_T3 + 1],
                                            state[:, S_BTH:S_BTH + 1])
                        em.inc(ins, "EVV", 1)
                        em.mark("EVV", ("selprobed", t))

                sink.append(("probe", t, p_probe))

            def p_mask(v, t=t):
                em.wait(v, "SEL", ("selp0", t))
                if taps:
                    c6 = 160 + 6 * (t % 16)
                    v.tensor_copy(state[:, c6:c6 + 4], state[:, 0:4])
                    v.tensor_copy(state[:, c6 + 4:c6 + 5],
                                  state[:, S_BTH:S_BTH + 1])
                    v.tensor_copy(state[:, c6 + 5:c6 + 6],
                                  state[:, S_BERR:S_BERR + 1])
                    v.tensor_copy(sml[:, 0:48], aT[:, 64:112])
                    v.tensor_copy(sml[:, 48:72], wrk[:, 64:88])
                    v.drain()
                v.scalar_tensor_tensor(wrk[:], aT[:],
                                       state[:, S_BTH:S_BTH + 1], wrk[:],
                                       ALU.is_ge, ALU.mult)
                if taps:
                    v.drain()
                    v.tensor_copy(sml[:, 72:96], wrk[:, 64:88])
                ins = v.tensor_copy(state[:, S_T3 + 1:S_T3 + 2],
                                    state[:, S_BTH:S_BTH + 1])
                em.inc(ins, "EVV", 1)
                em.mark("EVV", ("selmask", t))

            sink.append(("mask", t, p_mask))

        NBUF = 2

        def guv(b):
            return guB[:, b * NJ * 512:(b + 1) * NJ * 512].rearrange(
                "p (j w) -> p j w", j=NJ)

        def st1_load_chunk(g, half, c):
            gl = half * NCH + c
            if gl >= NBUF:
                ph, pc = divmod(gl - NBUF, NCH)
                em.wait(g, "MM", ("s1mm", ph, pc, NTH - 1))
            v = guv(gl % NBUF)
            ins = g.dma_start(
                out=v[:, :, 0:IC],
                in_=wgT[:, c * IC:(c + 1) * IC].rearrange("(j p) i -> p j i", p=128))
            em.inc(ins, "LDW", 16)
            ins = g.dma_start(
                out=v[:, :, IC:2 * IC],
                in_=wuT[:, c * IC:(c + 1) * IC].rearrange("(j p) i -> p j i", p=128))
            em.inc(ins, "LDW", 16)
            em.mark("LDW", ("s1ld", half, c))

        def emit_st1(half, sel_pieces, sync_sched):
            def g_loads(g):
                if half == 1:
                    em.wait(g, "MM", ("s1mm", 0, NCH - 1, NTH - 1))
                src = xT_in[:, half * THALF:(half + 1) * THALF].rearrange(
                    "(j p) t -> p j t", p=128)
                ins = g.dma_start(out=xTb[:].rearrange("p (j t) -> p j t", j=NJ),
                                  in_=src)
                em.inc(ins, "LDW", 16)
                em.mark("LDW", ("s1x", half))
                for c in range(NCH):
                    st1_load_chunk(g, half, c)
                    if half == 0:
                        if NBUF <= c < NBUF + 8:
                            k8 = c - NBUF
                            ins = g.dma_start(
                                out=wd_f16[k8 * (I // 8):(k8 + 1) * (I // 8), :],
                                in_=wdT[k8 * (I // 8):(k8 + 1) * (I // 8), :])
                            em.inc(ins, "LDW", 16)
                        if NBUF + 8 <= c < NBUF + 12:
                            k4 = c - NBUF - 8
                            ins = g.dma_start(
                                out=wb_f16[k4 * (D // 4):(k4 + 1) * (D // 4), :],
                                in_=wbT[k4 * (D // 4):(k4 + 1) * (D // 4), :])
                            em.inc(ins, "LDW", 16)
                        if c == NBUF + 12:
                            ins = g.dma_start(out=bias_f16, in_=bias_in)
                            em.inc(ins, "LDW", 16)
                if half == 0:
                    em.mark("LDW", "preconv")

            block.gpsimd(g_loads)

            def t_mms(t):
                em.wait(t, "LDW", ("s1x", half))
                for c in range(NCH):
                    v = guv((half * NCH + c) % NBUF)
                    em.wait(t, "LDW", ("s1ld", half, c))
                    for tt in range(NTH):
                        bank = pk[tt % 2]
                        gl = (half * NCH + c) * NTH + tt
                        if gl >= 2:
                            pg = gl - 2
                            ph, pr = divmod(pg, NCH * NTH)
                            pc, ptt = divmod(pr, NTH)
                            em.wait(t, "EVV", ("s1ev", ph, pc, ptt))
                        for j in range(NJ):
                            ins = t.matmul(
                                bank[:, 0:512],
                                xTb[:, j * THALF + tt * 128:
                                    j * THALF + (tt + 1) * 128],
                                v[:, j, :], start=(j == 0), stop=(j == NJ - 1))
                        em.inc(ins, "MM", 1)
                        em.mark("MM", ("s1mm", half, c, tt))

            block.tensor(t_mms)

            def s_silu(s):
                for c in range(NCH):
                    par = (half * NCH + c) % 2
                    for tt in range(NTH):
                        bank = pk[tt % 2]
                        em.wait(s, "MM", ("s1mm", half, c, tt))
                        gl = (half * NCH + c) * NTH + tt
                        if gl >= 2:
                            pg = gl - 2
                            ph, pr = divmod(pg, NCH * NTH)
                            pc, ptt = divmod(pr, NTH)
                            em.wait(s, "EVV", ("s1ev", ph, pc, ptt))
                        sr = sRing[:, (tt % 2) * IC:(tt % 2 + 1) * IC]
                        ins = s.activation(sr, bank[:, 0:IC],
                                           AF.Identity if os.environ.get('ZTAP') else AF.Silu)
                        em.inc(ins, "EVA", 1)
                        em.mark("EVA", ("s1silu", half, c, tt))

            block.scalar(s_silu)

            def v_evac(v):
                for c in range(NCH):
                    par = (half * NCH + c) % 2
                    for tt in range(NTH):
                        bank = pk[tt % 2]
                        em.wait(v, "EVA", ("s1silu", half, c, tt))
                        gl = half * NCH + c
                        if gl >= 2:
                            ph, pc = divmod(gl - 2, NCH)
                            em.wait(v, "ST", ("s1st", ph, pc))
                        sr = sRing[:, (tt % 2) * IC:(tt % 2 + 1) * IC]
                        v.tensor_tensor(
                            p0Strip[:, (par * NTH + tt) * IC:
                                    (par * NTH + tt + 1) * IC],
                            sr, bank[:, IC:2 * IC], ALU.mult)
                        v.scalar_tensor_tensor(xsc[:, (tt % 2) * IC:
                                                   (tt % 2 + 1) * IC],
                                               sr, -1.0, sr, ALU.mult, ALU.max)
                        v.drain()
                        ins = v.tensor_single_scalar(
                            aStrip[:, (par * NTH + tt) * IC:
                                   (par * NTH + tt + 1) * IC],
                            xsc[:, (tt % 2) * IC:(tt % 2 + 1) * IC],
                            ACENTER, ALU.subtract)
                        em.inc(ins, "EVV", 1)
                        em.mark("EVV", ("s1ev", half, c, tt))
                        if half == 1 and sel_pieces and tt % 2 == 0:
                            kind, pt, fn = sel_pieces.pop(0)
                            fn(v)
                            cc = max(0, c - 1)
                            if kind == "probe":
                                pieces_left = [p for p in sel_pieces
                                               if p[1] == pt and p[0] == "probe"]
                                if not pieces_left:
                                    sync_sched.setdefault(cc, []).append(
                                        lambda q, pt=pt: sel_ld_p0(q, pt))
                                    p0_done.add(pt)
                            if kind == "mask":
                                sync_sched.setdefault(cc, []).append(
                                    lambda q, pt=pt: sel_store(q, pt))
                                st_done.add(pt)
                                if pt + 1 in early_tiles:
                                    sync_sched.setdefault(cc, []).append(
                                        lambda q, pt=pt: sel_ld_a(q, pt + 1))
                                    lda_done.add(pt + 1)

            block.vector(v_evac)

            def sy_store(sy):
                for c in range(NCH):
                    par = (half * NCH + c) % 2
                    em.wait(sy, "EVV", ("s1ev", half, c, NTH - 1))
                    r0, r1 = half * THALF, (half + 1) * THALF
                    ins = sy.dma_start(
                        out=a_scr[r0:r1, c * IC:(c + 1) * IC].rearrange(
                            "(tt p) i -> p tt i", p=128),
                        in_=aStrip[:, par * NTH * IC:(par + 1) * NTH * IC].rearrange(
                            "p (tt i) -> p tt i", tt=NTH))
                    em.inc(ins, "ST", 16)
                    ins = sy.dma_start(
                        out=p0_scr[r0:r1, c * IC:(c + 1) * IC].rearrange(
                            "(tt p) i -> p tt i", p=128),
                        in_=p0Strip[:, par * NTH * IC:(par + 1) * NTH * IC].rearrange(
                            "p (tt i) -> p tt i", tt=NTH))
                    em.inc(ins, "ST", 16)
                    em.mark("ST", ("s1st", half, c))
                    for fn in sync_sched.get(c, []):
                        fn(sy)

            block.sync(sy_store)

        early_tiles = [t for t in range(min(NT, 8)) if (t + 1) * 128 <= THALF]
        if stages < 2 or os.environ.get("SELMODE", "") == "late":
            early_tiles = []
        late_tiles = [t for t in range(min(NT, 8)) if t not in early_tiles]
        if stages < 2 or os.environ.get("SELMODE", "") == "early":
            late_tiles = []
        early_pieces = []
        for t in early_tiles:
            sel_pieces_for_tile(t, early_pieces)
        sched = {}
        sched0 = {}
        lda_done, p0_done, st_done = set(), set(), set()
        if early_tiles:
            t0 = early_tiles[0]
            sched0.setdefault(NCH - 1, []).append(
                lambda q, t0=t0: sel_ld_a(q, t0))
            lda_done.add(t0)

        emit_st1(0, [], sched0)
        emit_st1(1, early_pieces, sched)
        if early_pieces:
            rest = list(early_pieces)
            early_pieces.clear()

            def v_left(v, ps=rest):
                for _, _, p in ps:
                    p(v)

            block.vector(v_left)
        for t in early_tiles:
            if t not in lda_done:
                block.sync(lambda q, t=t: sel_ld_a(q, t))
                lda_done.add(t)
            if t not in p0_done:
                block.sync(lambda q, t=t: sel_ld_p0(q, t))
                p0_done.add(t)
            if t not in st_done:
                block.sync(lambda q, t=t: sel_store(q, t))
                st_done.add(t)
        for t in late_tiles:
            block.sync(lambda q, t=t: sel_ld_a(q, t))
            lp = []
            sel_pieces_for_tile(t, lp)

            def v_lt(v, lp=lp):
                for _, _, p in lp:
                    p(v)

            block.vector(v_lt)
            block.sync(lambda q, t=t: sel_ld_p0(q, t))
            block.sync(lambda q, t=t: sel_store(q, t))

        def sc_tp_body(sc, kk):
            rr = kk * 512
            for t in range(4):
                em.wait(sc, "SEL", ("selstore", 4 * kk + t))
            for i in range(NI):
                if kk >= 1:
                    em.wait(sc, "MM", ("mmoct", kk - 1, 7, min(i // 8, NOCT - 1)))
                ins = sc.dma_start_transpose(
                    pTs[:, i * 512:(i + 1) * 512],
                    p_scr[rr:rr + 512, i * 128:(i + 1) * 128])
                em.inc(ins, "LDT", 16)
                if i % 8 == 7:
                    em.mark("LDT", ("tpo", kk, i // 8))
            ins = sc.dma_start_transpose(
                pTs[:, (NI - 1) * 512:NI * 512],
                p_scr[rr:rr + 512, (NI - 1) * 128:NI * 128])
            em.inc(ins, "LDT", 16)
            em.mark("LDT", ("tpo", kk, (NI - 1) // 8))
            em.mark("LDT", ("tp", kk))

        def ln_io_g(g, kk):
            rr = kk * 512

            def hstore(pidx):
                ptt, pds = pidx // 8, pidx % 8
                em.wait(g, "EVA", ("app", kk, pidx))
                ins = g.dma_start(
                    out=h_scr[rr + ptt * 128:rr + ptt * 128 + 128,
                              pds * 512:(pds + 1) * 512],
                    in_=hCh[:, (pidx % 2) * 512:(pidx % 2 + 1) * 512])
                em.inc(ins, "HS", 16)
                em.mark("HS", ("hs", kk, pidx))
                ins = g.dma_start(
                    out=h_f16[rr + ptt * 128:rr + ptt * 128 + 128,
                              pds * 512:(pds + 1) * 512],
                    in_=hCh[:, (pidx % 2) * 512:(pidx % 2 + 1) * 512])
                em.inc(ins, "STG", 16)
                em.mark("STG", ("hc", kk, pidx))

            for tt in range(4):
                for ds in range(8):
                    idx = tt * 8 + ds
                    em.wait(g, "ST", ("rst", kk, ds, tt))
                    if idx >= 2:
                        em.wait(g, "EVA", ("app", kk, idx - 2))
                    elif kk >= 1:
                        em.wait(g, "EVA", ("app", kk - 1, 30 + idx))
                    ins = g.dma_start(
                        out=rlCh[:, (idx % 2) * 512:(idx % 2 + 1) * 512],
                        in_=r_scr[rr + tt * 128:rr + tt * 128 + 128,
                                  ds * 512:(ds + 1) * 512])
                    em.inc(ins, "RL", 16)
                    em.mark("RL", ("rl", kk, idx))
                    if idx >= 1:
                        hstore(idx - 1)
            hstore(31)

        for k in range(NSLAB if stages >= 3 else 0):
            r0 = k * 512

            pend = []
            pend_sched = {}
            sel_cycle_tiles = []
            if k + 2 < NSLAB:
                sel_cycle_tiles = list(range(4 * (k + 2), 4 * (k + 3)))
                for t in sel_cycle_tiles:
                    sel_pieces_for_tile(t, pend)
                t0 = sel_cycle_tiles[0]
                pend_sched.setdefault(-1, []).append(
                    lambda q, t0=t0: sel_ld_a(q, t0))

            if k == 0:
                block.scalar(lambda sc: sc_tp_body(sc, 0))

            S2LVL = int(os.environ.get("S2LVL", "4"))

            def sy_main(sy, k=k):
                if S2LVL < 2:
                    return
                for fn in pend_sched.get(-1, []):
                    fn(sy)
                for ds in range(8):
                    for o_ in range(NOCT):
                        i0, i1 = o_ * 8, min(o_ * 8 + 8, NI)
                        idx = ds * NOCT + o_
                        if idx >= 2:
                            em.wait(sy, "MM", ("mmoct", k, (idx - 2) // NOCT,
                                               (idx - 2) % NOCT))
                        elif k >= 1:
                            em.wait(sy, "MM", ("mmoct", k - 1, 7, NOCT - 1))
                        src = wd_f16[i0 * 128:i1 * 128,
                                     ds * 512:(ds + 1) * 512].rearrange(
                                         "(b p) w -> p b w", p=128)
                        dst = wdR[:, (idx % 2) * 8 * 512:
                                  (idx % 2) * 8 * 512 + (i1 - i0) * 512].rearrange(
                                      "p (b w) -> p b w", b=i1 - i0)
                        ins = sy.dma_start(out=dst, in_=src)
                        em.inc(ins, "ST", 16)
                        em.mark("ST", ("wd", k, ds, o_))
                    if ds >= 1:
                        for tt in range(4):
                            em.wait(sy, "EVV", ("ev2", k, ds - 1, tt))
                            pidx = (ds - 1) * 4 + tt
                            ins = sy.dma_start(
                                out=r_scr[r0 + tt * 128:r0 + tt * 128 + 128,
                                          (ds - 1) * 512:ds * 512],
                                in_=rCh[:, (pidx % 2) * 512:(pidx % 2 + 1) * 512])
                            em.inc(ins, "ST", 16)
                            em.mark("ST", ("rst", k, ds - 1, tt))
                    for fn in (pend_sched.get(ds - 1, []) if ds >= 1 else []):
                        fn(sy)
                    for tt in range(4):
                        idx = ds * 4 + tt
                        if idx >= 2:
                            em.wait(sy, "EVV", ("ev2", k, (idx - 2) // 4,
                                                (idx - 2) % 4))
                        elif k >= 1:
                            em.wait(sy, "EVV", ("ev2", k - 1, 7, 2 + idx))
                        ins = sy.dma_start(
                            out=xCh[:, (idx % 2) * 512:(idx % 2 + 1) * 512],
                            in_=x_in[r0 + tt * 128:r0 + tt * 128 + 128,
                                     ds * 512:(ds + 1) * 512])
                        em.inc(ins, "ST", 16)
                        em.mark("ST", ("x2", k, ds, tt))
                for tt in range(4):
                    em.wait(sy, "EVV", ("ev2", k, 7, tt))
                    pidx = 7 * 4 + tt
                    ins = sy.dma_start(
                        out=r_scr[r0 + tt * 128:r0 + tt * 128 + 128, 7 * 512:8 * 512],
                        in_=rCh[:, (pidx % 2) * 512:(pidx % 2 + 1) * 512])
                    em.inc(ins, "ST", 16)
                    em.mark("ST", ("rst", k, 7, tt))
                for b in (7, 99):
                    for fn in pend_sched.get(b, []):
                        fn(sy)
                if os.environ.get("S2LN", "1") == "0":
                    return

            def t_2b(t, k=k):
                if S2LVL < 3:
                    return
                if k >= 1:
                    em.wait(t, "EVV", ("ev2", k - 1, 7, 3))
                for ds in range(8):
                    bset = pk[0:4] if ds % 2 == 0 else pk[4:8]
                    if ds >= 2:
                        em.wait(t, "EVV", ("ev2", k, ds - 2, 3))
                    for o_ in range(NOCT):
                        i0, i1 = o_ * 8, min(o_ * 8 + 8, NI)
                        idx = ds * NOCT + o_
                        if ds == 0 and o_ == 0:
                            em.wait(t, "LDT", ("tp", k))
                        em.wait(t, "ST", ("wd", k, ds, min(o_ + 1, NOCT - 1)))
                        for i in range(i0, i1):
                            wv = wdR[:, (idx % 2) * 8 * 512 + (i - i0) * 512:
                                     (idx % 2) * 8 * 512 + (i - i0 + 1) * 512]
                            for tt in range(4):
                                ins = t.matmul(
                                    bset[tt][:, 0:512],
                                    pTs[:, i * 512 + tt * 128:
                                        i * 512 + (tt + 1) * 128],
                                    wv, start=(i == 0), stop=(i == NI - 1))
                        em.inc(ins, "MM", 1)
                        em.mark("MM", ("mmoct", k, ds, o_))

            block.tensor(t_2b)

            def v_2b(v, k=k, pend=pend):
                if S2LVL < 4:
                    while pend:
                        pend.pop(0)[2](v)
                    return
                for ds in range(8):
                    bset = pk[0:4] if ds % 2 == 0 else pk[4:8]
                    for tt in range(4):
                        idx = ds * 4 + tt
                        em.wait(v, "MM", ("mmoct", k, ds, NOCT - 1))
                        em.wait(v, "ST", ("x2", k, ds, min(tt + 1, 3)))
                        if idx >= 2:
                            em.wait(v, "ST", ("rst", k, (idx - 2) // 4,
                                              (idx - 2) % 4))
                        elif k >= 1:
                            em.wait(v, "ST", ("rst", k - 1, 7, 2 + idx))
                        cp = idx % 2
                        acc1 = (state[:, S_SUM + tt * 8 + ds:
                                      S_SUM + tt * 8 + ds + 1]
                                if os.environ.get("S2ACC", "1") == "1" else None)
                        ins = v.scalar_tensor_tensor(
                            rCh[:, cp * 512:(cp + 1) * 512],
                            bset[tt][:, 0:512], 1.0,
                            xCh[:, cp * 512:(cp + 1) * 512],
                            ALU.mult, ALU.add,
                            accum_out=acc1)
                        if os.environ.get("S2SQ", "1") == "1":
                            acc2 = (state[:, S_SQ + tt * 8 + ds:
                                          S_SQ + tt * 8 + ds + 1]
                                    if os.environ.get("S2ACC", "1") == "1" else None)
                            ins = v.scalar_tensor_tensor(
                                sqJ[:], rCh[:, cp * 512:(cp + 1) * 512], 1.0,
                                rCh[:, cp * 512:(cp + 1) * 512],
                                ALU.mult, ALU.mult,
                                accum_out=acc2)
                        em.inc(ins, "EVV", 1)
                        em.mark("EVV", ("ev2", k, ds, tt))
                    for _ in range(7):
                        if pend:
                            kind, pt, fn = pend.pop(0)
                            fn(v)
                            if kind == "probe" and not any(
                                    p[1] == pt and p[0] == "probe" for p in pend):
                                pend_sched.setdefault(ds, []).append(
                                    lambda q, pt=pt: sel_ld_p0(q, pt))
                            if kind == "mask":
                                pend_sched.setdefault(ds, []).append(
                                    lambda q, pt=pt: sel_store(q, pt))
                                if pt + 1 in sel_cycle_tiles:
                                    pend_sched.setdefault(ds, []).append(
                                        lambda q, pt=pt: sel_ld_a(q, pt + 1))
                while pend:
                    kind, pt, fn = pend.pop(0)
                    fn(v)
                    if kind == "probe" and not any(
                            p[1] == pt and p[0] == "probe" for p in pend):
                        pend_sched.setdefault(99, []).append(
                            lambda q, pt=pt: sel_ld_p0(q, pt))
                    if kind == "mask":
                        pend_sched.setdefault(99, []).append(
                            lambda q, pt=pt: sel_store(q, pt))
                        if pt + 1 in sel_cycle_tiles:
                            pend_sched.setdefault(99, []).append(
                                lambda q, pt=pt: sel_ld_a(q, pt + 1))
                if os.environ.get("S2LN", "1") == "0":
                    return
                if k >= 2:
                    em.wait(v, "EVA", ("app", k - 2, 31))
                v.drain()
                MUB = S_MU if k % 2 == 0 else S_MU_O
                RSB = S_RS if k % 2 == 0 else S_RS_O
                for tt in range(4):
                    mu = state[:, MUB + tt:MUB + tt + 1]
                    va = state[:, S_T1:S_T1 + 1]
                    v.tensor_reduce(mu, state[:, S_SUM + tt * 8:S_SUM + tt * 8 + 8],
                                    AX.X, ALU.add)
                    v.tensor_reduce(va, state[:, S_SQ + tt * 8:S_SQ + tt * 8 + 8],
                                    AX.X, ALU.add)
                    v.drain()
                    v.tensor_single_scalar(mu, mu, 1.0 / D, ALU.mult)
                    v.tensor_single_scalar(va, va, 1.0 / D, ALU.mult)
                    v.drain()
                    v.tensor_tensor(state[:, S_T2:S_T2 + 1], mu, mu, ALU.mult)
                    v.drain()
                    v.tensor_tensor(va, va, state[:, S_T2:S_T2 + 1], ALU.subtract)
                    v.drain()
                    ins = v.tensor_single_scalar(
                        state[:, RSB + tt:RSB + tt + 1], va, LN_EPS, ALU.add)
                    em.inc(ins, "EVV", 1)
                    em.mark("EVV", ("var", k, tt))

            block.vector(v_2b)
            block.sync(sy_main)

            if k + 1 < NSLAB:
                block.scalar(lambda sc, kk=k + 1: sc_tp_body(sc, kk))

            def s_ln(s, k=k):
                RSB = S_RS if k % 2 == 0 else S_RS_O
                for tt in range(4):
                    em.wait(s, "EVV", ("var", k, tt))
                    ins = s.activation(state[:, RSB + tt:RSB + tt + 1],
                                       state[:, RSB + tt:RSB + tt + 1], AF.Sqrt)
                    em.inc(ins, "EVA", 1)
                    em.mark("EVA", ("sqrt", k, tt))

            if os.environ.get('S2LN', '1') == '1':
                block.scalar(s_ln)

            def v_fin(v, k=k):
                MUB = S_MU if k % 2 == 0 else S_MU_O
                RSB = S_RS if k % 2 == 0 else S_RS_O
                BIB = S_BI if k % 2 == 0 else S_BI_O
                for tt in range(4):
                    em.wait(v, "EVA", ("sqrt", k, tt))
                    rs = state[:, RSB + tt:RSB + tt + 1]
                    v.reciprocal(rs, rs)
                    v.drain()
                    ins = v.scalar_tensor_tensor(
                        state[:, BIB + tt:BIB + tt + 1],
                        state[:, MUB + tt:MUB + tt + 1],
                        -1.0, rs, ALU.mult, ALU.mult)
                    em.inc(ins, "EVV", 1)
                    em.mark("EVV", ("fin", k, tt))

            if os.environ.get('S2LN', '1') == '1':
                block.vector(v_fin)

            def s_apply(s, k=k):
                RSB = S_RS if k % 2 == 0 else S_RS_O
                BIB = S_BI if k % 2 == 0 else S_BI_O
                for tt in range(4):
                    em.wait(s, "EVV", ("fin", k, tt))
                    for ds in range(8):
                        idx = tt * 8 + ds
                        em.wait(s, "RL", ("rl", k, idx))
                        if idx >= 2:
                            em.wait(s, "HS", ("hs", k, idx - 2))
                            em.wait(s, "STG", ("hc", k, idx - 2))
                        elif k >= 1:
                            em.wait(s, "HS", ("hs", k - 1, 30 + idx))
                            em.wait(s, "STG", ("hc", k - 1, 30 + idx))
                        ins = s.activation(
                            hCh[:, (idx % 2) * 512:(idx % 2 + 1) * 512],
                            rlCh[:, (idx % 2) * 512:(idx % 2 + 1) * 512],
                            AF.Identity,
                            bias=state[:, BIB + tt:BIB + tt + 1],
                            scale=state[:, RSB + tt:RSB + tt + 1])
                        em.inc(ins, "EVA", 1)
                        em.mark("EVA", ("app", k, idx))

            if os.environ.get('S2LN', '1') == '1':
                block.scalar(s_apply)

            if os.environ.get('S2LN', '1') == '1' and k >= 1:
                block.gpsimd(lambda g, kk=k - 1: ln_io_g(g, kk))

        if stages >= 3 and NSLAB >= 1 and os.environ.get("S2LN", "1") == "1" \
                and int(os.environ.get("S2LVL", "4")) >= 2:
            block.gpsimd(lambda g: ln_io_g(g, NSLAB - 1))

        def g_smalls(g):
            em.wait(g, "MM", ("mmoct", NSLAB - 1, 7, NOCT - 1))
            em.inc(g.dma_start(out=expB[:], in_=expm_in), "LDW", 16)
            em.inc(g.dma_start(out=onesB[:], in_=ones_in), "LDW", 16)
            em.inc(g.dma_start(out=gwB[:], in_=gwP), "LDW", 16)
            em.inc(g.dma_start(out=laB[:].rearrange("p (j r) -> p j r", j=NJ),
                               in_=laT.rearrange("(j p) r -> p j r", p=128)),
                   "LDW", 16)
            em.inc(g.dma_start(out=lbB[:], in_=lbT), "LDW", 16)
            em.mark("LDW", "smalls")

        if stages >= 4:
            block.gpsimd(g_smalls)

        for k in range(NSLAB if stages >= 4 else 0):
            r0 = k * 512

            def sc_htp(sc, k=k):
                for idx in range(32):
                    em.wait(sc, "STG", ("hc", k, idx))
                if k >= 1:
                    em.wait(sc, "MM", ("mm4fin", k - 1))
                else:
                    em.wait(sc, "MM", ("mmoct", NSLAB - 1, 7, NOCT - 1))
                    em.wait(sc, "SEL", ("selstore", NT - 1))
                for j in range(NJ):
                    ins = sc.dma_start_transpose(
                        hTb[:, j * 512:(j + 1) * 512],
                        h_f16[r0:r0 + 512, j * 128:(j + 1) * 128])
                    em.inc(ins, "LDT", 16)
                em.mark("LDT", ("htp", k))

            block.scalar(sc_htp)

            def t_p1(t, k=k):
                em.wait(t, "LDW", "smalls")
                em.wait(t, "LDW", "ident")
                em.wait(t, "LDT", ("htp", k))
                if k >= 1:
                    em.wait(t, "EVV", ("ev4", k - 1, 7, 3))
                else:
                    em.wait(t, "EVV", ("ev2", NSLAB - 1, 7, 3))
                for j in range(NJ):
                    ins = t.matmul(pk[0][:, 0:512], gwB[:, j * 128:(j + 1) * 128],
                                   hTb[:, j * 512:(j + 1) * 512],
                                   start=(j == 0), stop=(j == NJ - 1))
                em.inc(ins, "MM", 1)
                em.mark("MM", ("mmgl", k))
                for j in range(NJ):
                    ins = t.matmul(pk[4][:, 0:512], laB[:, j * 128:(j + 1) * 128],
                                   hTb[:, j * 512:(j + 1) * 512],
                                   start=(j == 0), stop=(j == NJ - 1))
                em.inc(ins, "MM", 1)
                em.mark("MM", ("mmlow", k))

            block.tensor(t_p1)

            def s_p1(s, k=k):
                em.wait(s, "MM", ("mmgl", k))
                ins = s.copy(glT[:], pk[0][0:8, 0:512])
                em.inc(ins, "EVA", 1)
                em.mark("EVA", ("glev", k))

            block.scalar(s_p1)

            def t_p2(t, k=k):
                em.wait(t, "EVA", ("glev", k))
                for tt in range(4):
                    ins = t.transpose(pk[1][:, tt * 8:(tt + 1) * 8],
                                      glT[0:8, tt * 128:(tt + 1) * 128],
                                      ident32[0:8, 0:8])
                em.inc(ins, "MM", 1)
                em.mark("MM", ("mmglt", k))

            block.tensor(t_p2)

            def v_p1(v, k=k):
                if k == 0:
                    v.memset(biasOC[:], 0.0)
                    v.drain()
                em.wait(v, "MM", ("mmglt", k))
                v.tensor_copy(glog[:], pk[1][:, 0:32])
                v.drain()
                for tt in range(4):
                    g8 = glog[:, tt * 8:(tt + 1) * 8]
                    m1 = state[:, S_M1 + tt:S_M1 + tt + 1]
                    m2 = state[:, S_M2 + tt:S_M2 + tt + 1]
                    g2 = sml[:, tt * 8:tt * 8 + 8]
                    mk1 = sml[:, 64 + tt * 8:64 + tt * 8 + 8]
                    v.tensor_reduce(m1, g8, AX.X, ALU.max)
                    v.drain()
                    v.tensor_scalar(mk1, g8, m1, None, ALU.is_ge)
                    v.drain()
                    v.scalar_tensor_tensor(g2, mk1, -1e30, g8, ALU.mult, ALU.add)
                    v.drain()
                    v.tensor_reduce(m2, g2, AX.X, ALU.max)
                    v.drain()
                    ins = v.tensor_tensor(m2, m2, m1, ALU.subtract)
                    v.drain()
                em.inc(ins, "EVV", 1)
                em.mark("EVV", ("m12", k))

            block.vector(v_p1)

            def s_p2(s, k=k):
                em.wait(s, "EVV", ("m12", k))
                ins = s.activation(state[:, S_M2:S_M2 + 4], state[:, S_M2:S_M2 + 4],
                                   AF.Exp)
                em.inc(ins, "EVA", 1)
                em.mark("EVA", ("exp", k))

            block.scalar(s_p2)

            def v_p2(v, k=k):
                em.wait(v, "EVA", ("exp", k))
                for tt in range(4):
                    g8 = glog[:, tt * 8:(tt + 1) * 8]
                    g2 = sml[:, tt * 8:tt * 8 + 8]
                    mk1 = sml[:, 64 + tt * 8:64 + tt * 8 + 8]
                    mk2 = sml[:, 128 + tt * 8:128 + tt * 8 + 8]
                    ee = state[:, S_M2 + tt:S_M2 + tt + 1]
                    w1 = state[:, S_T1:S_T1 + 1]
                    w2 = state[:, S_T2:S_T2 + 1]
                    m2v = state[:, S_T3:S_T3 + 1]
                    v.tensor_single_scalar(w1, ee, 1.0, ALU.add)
                    v.drain()
                    v.reciprocal(w1, w1)
                    v.drain()
                    v.tensor_single_scalar(w1, w1, SCALING, ALU.mult)
                    v.drain()
                    v.tensor_single_scalar(w2, w1, -1.0, ALU.mult)
                    v.drain()
                    v.tensor_single_scalar(w2, w2, SCALING, ALU.add)
                    v.tensor_reduce(m2v, g2, AX.X, ALU.max)
                    v.drain()
                    v.tensor_scalar(mk2, g2, m2v, None, ALU.is_ge)
                    v.drain()
                    v.tensor_scalar(mk1, mk1, w1, None, ALU.mult)
                    v.tensor_scalar(mk2, mk2, w2, None, ALU.mult)
                    v.drain()
                    ins = v.tensor_tensor(wful[:, tt * 8:(tt + 1) * 8], mk1, mk2,
                                          ALU.add)
                    v.drain()
                em.inc(ins, "EVV", 1)
                em.mark("EVV", ("wful", k))

            block.vector(v_p2)

            def t_p3(t, k=k):
                em.wait(t, "EVV", ("wful", k))
                for tt in range(4):
                    ins = t.transpose(pk[2][0:8, tt * 128:(tt + 1) * 128],
                                      wful[:, tt * 8:(tt + 1) * 8], ident32[:])
                em.inc(ins, "MM", 1)
                em.mark("MM", ("mmwft", k))

            block.tensor(t_p3)

            def s_p3(s, k=k):
                em.wait(s, "MM", ("mmwft", k))
                ins = s.copy(wfT[:], pk[2][0:8, 0:512])
                em.inc(ins, "EVA", 1)
                em.mark("EVA", ("wftev", k))

            block.scalar(s_p3)

            def t_p4(t, k=k):
                em.wait(t, "EVA", ("wftev", k))
                ins = t.matmul(pk[3][:, 0:512], expB[:], wfT[:],
                               start=True, stop=True)
                em.inc(ins, "MM", 1)
                em.mark("MM", ("mmwex", k))

            block.tensor(t_p4)

            def v_p3(v, k=k):
                em.wait(v, "MM", ("mmlow", k))
                v.tensor_copy(lowS[:], pk[4][:, 0:512])
                v.drain()
                em.wait(v, "MM", ("mmwex", k))
                ins = v.tensor_tensor(lsT[:], lowS[:], pk[3][:, 0:512], ALU.mult)
                em.inc(ins, "EVV", 1)
                em.mark("EVV", ("lsT", k))

            block.vector(v_p3)

            if taps:
                def sy_dbg4(g, k=k):
                    em.wait(g, "EVV", ("lsT", k))
                    ins = g.dma_start(out=dbg4[k, :, 0:512], in_=lsT[:])
                    em.inc(ins, "STG", 16)
                    ins = g.dma_start(out=dbg4[k, :, 512:544], in_=glog[:])
                    em.inc(ins, "STG", 16)
                    ins = g.dma_start(out=dbg4[k, :, 544:576], in_=wful[:])
                    em.inc(ins, "STG", 16)
                    ins = g.dma_start(out=dbg4[k, :, 576:1088], in_=lowS[:])
                    em.inc(ins, "STG", 16)
                    ins = g.dma_start(out=dbg4[k, :, 1088:1120],
                                      in_=hTb[:, 0:32])
                    em.inc(ins, "STG", 16)
                    ins = g.dma_start(out=dbg4[k, :, 1120:1152],
                                      in_=laB[:, 2048:2080])
                    em.inc(ins, "STG", 16)

                block.gpsimd(sy_dbg4)

            def sy_4(sy, k=k):
                if os.environ.get("S4SER", "0") == "1":
                    em.wait(sy, "EVV", ("lsT", k))
                if k == 0:
                    em.wait(sy, "MM", ("mmoct", NSLAB - 1, 7, NOCT - 1))

                def wbload(oc):
                    if oc >= 2:
                        em.wait(sy, "MM", ("mm4", k, oc - 2, 3))
                    elif k >= 1:
                        em.wait(sy, "MM", ("mm4", k - 1, 6 + oc, 3))
                    src = wb_f16[:, oc * 512:(oc + 1) * 512].rearrange(
                        "(j p) w -> p j w", p=128)
                    dst = wbB[:, (oc % 2) * NJ * 512:
                              ((oc % 2) + 1) * NJ * 512].rearrange(
                                  "p (j w) -> p j w", j=NJ)
                    ins = sy.dma_start(out=dst, in_=src)
                    em.inc(ins, "ST", 16)
                    ins = sy.dma_start(
                        out=biasOC[0:1, (oc % 2) * 512:(oc % 2 + 1) * 512],
                        in_=bias_f16[0:1, oc * 512:(oc + 1) * 512])
                    em.inc(ins, "ST", 16)
                    em.mark("ST", ("wb", k, oc))

                wbload(0)
                for oc in range(8):
                    if oc + 1 < 8:
                        wbload(oc + 1)
                    if oc >= 1:
                        for tt in range(4):
                            em.wait(sy, "EVV", ("ev4", k, oc - 1, tt))
                            pidx = (oc - 1) * 4 + tt
                            ins = sy.dma_start(
                                out=y_scr[r0 + tt * 128:r0 + tt * 128 + 128,
                                          (oc - 1) * 512:oc * 512],
                                in_=yCh[:, (pidx % 2) * 512:(pidx % 2 + 1) * 512])
                            em.inc(ins, "ST", 16)
                            em.mark("ST", ("yst", k, oc - 1, tt))
                    for tt in range(4):
                        idx = oc * 4 + tt
                        if idx >= 2:
                            em.wait(sy, "EVV", ("ev4", k, (idx - 2) // 4,
                                                (idx - 2) % 4))
                        elif k == 0:
                            em.wait(sy, "EVV", ("ev2", NSLAB - 1, 7, 3))
                        else:
                            em.wait(sy, "EVV", ("ev4", k - 1, 7, 2 + idx))
                        em.wait(sy, "HS", ("hs", k, tt * 8 + oc))
                        ins = sy.dma_start(
                            out=hRes[:, (idx % 2) * 512:(idx % 2 + 1) * 512],
                            in_=h_scr[r0 + tt * 128:r0 + tt * 128 + 128,
                                      oc * 512:(oc + 1) * 512])
                        em.inc(ins, "ST", 16)
                        em.mark("ST", ("h4", k, oc, tt))
                for tt in range(4):
                    em.wait(sy, "EVV", ("ev4", k, 7, tt))
                    pidx = 7 * 4 + tt
                    ins = sy.dma_start(
                        out=y_scr[r0 + tt * 128:r0 + tt * 128 + 128,
                                  7 * 512:8 * 512],
                        in_=yCh[:, (pidx % 2) * 512:(pidx % 2 + 1) * 512])
                    em.inc(ins, "ST", 16)
                    em.mark("ST", ("yst", k, 7, tt))
                for tt in range(4):
                    for ds in range(8):
                        idx = tt * 8 + ds
                        em.wait(sy, "ST", ("yst", k, ds, tt))
                        if idx >= 2:
                            em.wait(sy, "EVA", ("app4", k, idx - 2))
                        elif k == 0:
                            em.wait(sy, "EVA", ("app", NSLAB - 1, 31))
                        else:
                            em.wait(sy, "EVA", ("app4", k - 1, 30 + idx))
                        ins = sy.dma_start(
                            out=ylCh[:, (idx % 2) * 512:(idx % 2 + 1) * 512],
                            in_=y_scr[r0 + tt * 128:r0 + tt * 128 + 128,
                                      ds * 512:(ds + 1) * 512])
                        em.inc(ins, "ST", 16)
                        em.mark("ST", ("yl", k, idx))
                        if idx >= 1:
                            pidx = idx - 1
                            ptt, pds = pidx // 8, pidx % 8
                            em.wait(sy, "EVA", ("app4", k, pidx))
                            ins = sy.dma_start(
                                out=y_out[r0 + ptt * 128:r0 + ptt * 128 + 128,
                                          pds * 512:(pds + 1) * 512],
                                in_=yoCh[:, (pidx % 2) * 512:(pidx % 2 + 1) * 512])
                            em.inc(ins, "ST", 16)
                            em.mark("ST", ("yo", k, pidx))
                em.wait(sy, "EVA", ("app4", k, 31))
                ins = sy.dma_start(
                    out=y_out[r0 + 3 * 128:r0 + 4 * 128, 7 * 512:8 * 512],
                    in_=yoCh[:, (31 % 2) * 512:(31 % 2 + 1) * 512])
                em.inc(ins, "ST", 16)
                em.mark("ST", ("yo", k, 31))

            block.sync(sy_4)

            def t_4(t, k=k):
                em.wait(t, "EVV", ("lsT", k))
                for oc in range(8):
                    bset = pk[0:4] if oc % 2 == 0 else pk[4:8]
                    em.wait(t, "ST", ("wb", k, min(oc + 1, 7)))
                    if oc >= 2:
                        em.wait(t, "EVV", ("ev4", k, oc - 2, 3))
                    wbv = wbB[:, (oc % 2) * NJ * 512:((oc % 2) + 1) * NJ * 512]
                    for tt in range(4):
                        for j in range(NJ):
                            t.matmul(bset[tt][:, 0:512],
                                     hTb[:, j * 512 + tt * 128:
                                         j * 512 + (tt + 1) * 128],
                                     wbv[:, j * 512:(j + 1) * 512],
                                     start=(j == 0), stop=False)
                        t.matmul(bset[tt][:, 0:512], onesB[:],
                                 biasOC[:, (oc % 2) * 512:(oc % 2 + 1) * 512],
                                 start=False, stop=False)
                        ins = t.matmul(bset[tt][:, 0:512],
                                       lsT[:, tt * 128:(tt + 1) * 128],
                                       lbB[:, oc * 512:(oc + 1) * 512],
                                       start=False, stop=True)
                        em.inc(ins, "MM", 1)
                        em.mark("MM", ("mm4", k, oc, tt))
                em.mark("MM", ("mm4fin", k))

            block.tensor(t_4)

            def v_4(v, k=k):
                for oc in range(8):
                    bset = pk[0:4] if oc % 2 == 0 else pk[4:8]
                    for tt in range(4):
                        idx = oc * 4 + tt
                        em.wait(v, "MM", ("mm4", k, oc, tt))
                        em.wait(v, "ST", ("h4", k, oc, min(tt + 1, 3)))
                        if idx >= 2:
                            em.wait(v, "ST", ("yst", k, (idx - 2) // 4,
                                              (idx - 2) % 4))
                        elif k == 0:
                            em.wait(v, "ST", ("rst", NSLAB - 1, 7, 3))
                        else:
                            em.wait(v, "ST", ("yst", k - 1, 7, 2 + idx))
                        cp = idx % 2
                        v.scalar_tensor_tensor(
                            yCh[:, cp * 512:(cp + 1) * 512],
                            bset[tt][:, 0:512], 1.0,
                            hRes[:, cp * 512:(cp + 1) * 512],
                            ALU.mult, ALU.add,
                            accum_out=state[:, S_SUM + tt * 8 + oc:
                                            S_SUM + tt * 8 + oc + 1])
                        ins = v.scalar_tensor_tensor(
                            sqJ4[:], yCh[:, cp * 512:(cp + 1) * 512], 1.0,
                            yCh[:, cp * 512:(cp + 1) * 512],
                            ALU.mult, ALU.mult,
                            accum_out=state[:, S_SQ + tt * 8 + oc:
                                            S_SQ + tt * 8 + oc + 1])
                        em.inc(ins, "EVV", 1)
                        em.mark("EVV", ("ev4", k, oc, tt))
                if k >= 2:
                    em.wait(v, "EVA", ("app4", k - 2, 31))
                v.drain()
                MUB = S_MU2_P[k % 2]
                RSB = S_RS2_P[k % 2]
                for tt in range(4):
                    mu = state[:, MUB + tt:MUB + tt + 1]
                    va = state[:, S_T1:S_T1 + 1]
                    v.tensor_reduce(mu, state[:, S_SUM + tt * 8:S_SUM + tt * 8 + 8],
                                    AX.X, ALU.add)
                    v.tensor_reduce(va, state[:, S_SQ + tt * 8:S_SQ + tt * 8 + 8],
                                    AX.X, ALU.add)
                    v.drain()
                    v.tensor_single_scalar(mu, mu, 1.0 / D, ALU.mult)
                    v.tensor_single_scalar(va, va, 1.0 / D, ALU.mult)
                    v.drain()
                    v.tensor_tensor(state[:, S_T2:S_T2 + 1], mu, mu, ALU.mult)
                    v.drain()
                    v.tensor_tensor(va, va, state[:, S_T2:S_T2 + 1], ALU.subtract)
                    v.drain()
                    ins = v.tensor_single_scalar(
                        state[:, RSB + tt:RSB + tt + 1], va, LN_EPS, ALU.add)
                    em.inc(ins, "EVV", 1)
                    em.mark("EVV", ("var4", k, tt))

            block.vector(v_4)

            def s_ln2(s, k=k):
                RSB = S_RS2_P[k % 2]
                for tt in range(4):
                    em.wait(s, "EVV", ("var4", k, tt))
                    ins = s.activation(state[:, RSB + tt:RSB + tt + 1],
                                       state[:, RSB + tt:RSB + tt + 1], AF.Sqrt)
                    em.inc(ins, "EVA", 1)
                    em.mark("EVA", ("sqrt4", k, tt))

            block.scalar(s_ln2)

            def v_fin2(v, k=k):
                MUB = S_MU2_P[k % 2]
                RSB = S_RS2_P[k % 2]
                BIB = S_BI2_P[k % 2]
                for tt in range(4):
                    em.wait(v, "EVA", ("sqrt4", k, tt))
                    rs = state[:, RSB + tt:RSB + tt + 1]
                    v.reciprocal(rs, rs)
                    v.drain()
                    ins = v.scalar_tensor_tensor(
                        state[:, BIB + tt:BIB + tt + 1],
                        state[:, MUB + tt:MUB + tt + 1],
                        -1.0, rs, ALU.mult, ALU.mult)
                    em.inc(ins, "EVV", 1)
                    em.mark("EVV", ("fin4", k, tt))

            block.vector(v_fin2)

            def s_apply2(s, k=k):
                RSB = S_RS2_P[k % 2]
                BIB = S_BI2_P[k % 2]
                for tt in range(4):
                    em.wait(s, "EVV", ("fin4", k, tt))
                    for ds in range(8):
                        idx = tt * 8 + ds
                        em.wait(s, "ST", ("yl", k, idx))
                        if idx >= 2:
                            em.wait(s, "ST", ("yo", k, idx - 2))
                        elif k == 0:
                            em.wait(s, "HS", ("hs", NSLAB - 1, 31))
                            em.wait(s, "STG", ("hc", NSLAB - 1, 31))
                        else:
                            em.wait(s, "ST", ("yo", k - 1, 30 + idx))
                        ins = s.activation(
                            yoCh[:, (idx % 2) * 512:(idx % 2 + 1) * 512],
                            ylCh[:, (idx % 2) * 512:(idx % 2 + 1) * 512],
                            AF.Identity,
                            bias=state[:, BIB + tt:BIB + tt + 1],
                            scale=state[:, RSB + tt:RSB + tt + 1])
                        em.inc(ins, "EVA", 1)
                        em.mark("EVA", ("app4", k, idx))

            block.scalar(s_apply2)

        def sy_fin(sy):
            em.waitv(sy, "RL", em.n["RL"])
            em.waitv(sy, "HS", em.n["HS"])
            em.waitv(sy, "SEL", em.n["SEL"])
            em.waitv(sy, "ST", em.n["ST"])
            em.waitv(sy, "STG", em.n["STG"])
            em.waitv(sy, "LDT", em.n["LDT"])
            em.waitv(sy, "LDW", em.n["LDW"])

        block.sync(sy_fin)

    return nc


def build_program(n_tok=2048, taps=False, stages=4):
    rec = {}
    _emit(n_tok, taps, rec, final=False, stages=stages)
    return _emit(n_tok, taps, rec, final=True, stages=stages)


def _pack_gw(gwT):
    out = np.zeros((128, NJ * 128), dtype=np.float32)
    g = gwT.reshape(NJ, 128, 8)
    for j in range(NJ):
        out[:, j * 128:j * 128 + 8] = g[j]
    return out


def kernel(L0_attn_x, L0_attn_w_gate, L0_attn_w_up, L0_attn_w_down,
           L0_ffn_gate_w, L0_ffn_base_w, L0_ffn_base_b,
           L0_ffn_lora_A, L0_ffn_lora_B):
    n_cores = 8
    B, S, _ = L0_attn_x.shape
    ntok = B * S
    per = ntok // n_cores
    x2 = np.ascontiguousarray(np.asarray(L0_attn_x, dtype=np.float32).reshape(ntok, D))

    def T(a):
        return np.ascontiguousarray(np.asarray(a, dtype=np.float32).T)

    consts = {
        "wgT": T(L0_attn_w_gate), "wuT": T(L0_attn_w_up), "wdT": T(L0_attn_w_down),
        "wbT": T(L0_ffn_base_w),
        "gwP": _pack_gw(T(L0_ffn_gate_w)),
        "laT": T(np.asarray(L0_ffn_lora_A, dtype=np.float32).reshape(128, D)),
        "lbT": np.ascontiguousarray(
            np.transpose(np.asarray(L0_ffn_lora_B, dtype=np.float32),
                         (0, 2, 1)).reshape(128, D)),
        "bias": np.asarray(L0_ffn_base_b, dtype=np.float32).reshape(1, D),
        "ident": np.eye(128, dtype=np.float32),
        "expm": np.repeat(np.eye(8, dtype=np.float32), 16, axis=1).reshape(8, 128),
        "ones": np.concatenate([np.ones((1, 128), np.float32),
                                np.zeros((127, 128), np.float32)]),
    }
    nc = build_program(n_tok=per, taps=False)
    in_maps = []
    for c in range(n_cores):
        xs = x2[c * per:(c + 1) * per]
        m = {"x": xs, "xT": np.ascontiguousarray(xs.T)}
        m.update(consts)
        in_maps.append(m)
    res = run_bass_kernel_spmd(nc, in_maps, core_ids=list(range(n_cores)))
    y = np.concatenate([np.asarray(r["y"], dtype=np.float32) for r in res.results],
                       axis=0)
    return y.reshape(B, S, D)


# version 03
# speedup vs baseline: 1.0886x; 1.0886x over previous
import os
import numpy as np

import concourse.bass as bass
import concourse.mybir as mybir
from concourse.bass_utils import run_bass_kernel_spmd

F32 = mybir.dt.float32
F16 = mybir.dt.float16
AF = mybir.ActivationFunctionType
ALU = mybir.AluOpType
AX = mybir.AxisListType

D = 4096
I = 11008
NJ = D // 128
NI = I // 128
IC = 256
NCH = I // IC
NOCT = (NI + 7) // 8
KSEL = 5504.0
SUBW = 2048
KSUB = KSEL * SUBW / I
CSCALE = float(I) / SUBW
NPS = 7
NPF = 5
NPROBE = NPS + NPF
LN_EPS = 1e-5
SCALING = 2.0
ACENTER = 0.28125
SEMS = ["LDW", "LDT", "SEL", "MM", "EVV", "EVA", "ST", "STG", "RL", "HS"]


class Em:
    def __init__(self, nc, rec, final):
        self.nc, self.rec, self.final = nc, rec, final
        self.h, self.n = {}, {}
        for nm in SEMS:
            self.h[nm] = nc.semaphore(nm).__enter__()
            self.n[nm] = 0

    def inc(self, ins, sem, by=1):
        self.n[sem] += by
        ins.then_inc(self.h[sem], by)

    def mark(self, sem, key):
        if self.final:
            assert self.rec[key] == self.n[sem], (key, self.rec[key], self.n[sem])
        else:
            self.rec[key] = self.n[sem]

    def wait(self, eng, sem, key):
        if self.final:
            eng.wait_ge(self.h[sem], self.rec[key])

    def waitv(self, eng, sem, v):
        if self.final:
            eng.wait_ge(self.h[sem], v)


def _emit(n_tok, taps, rec, final, stages=4):
    NT = n_tok // 128
    NSLAB = n_tok // 512
    THALF = n_tok // 2
    NTH = THALF // 128
    assert NTH % 2 == 0 and NSLAB >= 1

    nc = bass.Bass("TRN2", target_bir_lowering=False)

    x_in = nc.dram_tensor("x", [n_tok, D], F32, kind="ExternalInput").ap()
    xT_in = nc.dram_tensor("xT", [D, n_tok], F32, kind="ExternalInput").ap()
    wgT = nc.dram_tensor("wgT", [D, I], F32, kind="ExternalInput").ap()
    wuT = nc.dram_tensor("wuT", [D, I], F32, kind="ExternalInput").ap()
    wdT = nc.dram_tensor("wdT", [I, D], F32, kind="ExternalInput").ap()
    wbT = nc.dram_tensor("wbT", [D, D], F32, kind="ExternalInput").ap()
    gwP = nc.dram_tensor("gwP", [128, NJ * 128], F32, kind="ExternalInput").ap()
    laT = nc.dram_tensor("laT", [D, 128], F32, kind="ExternalInput").ap()
    lbT = nc.dram_tensor("lbT", [128, D], F32, kind="ExternalInput").ap()
    bias_in = nc.dram_tensor("bias", [1, D], F32, kind="ExternalInput").ap()
    ident_in = nc.dram_tensor("ident", [128, 128], F32, kind="ExternalInput").ap()
    expm_in = nc.dram_tensor("expm", [8, 128], F32, kind="ExternalInput").ap()
    ones_in = nc.dram_tensor("ones", [128, 128], F32, kind="ExternalInput").ap()

    y_out = nc.dram_tensor("y", [n_tok, D], F32, kind="ExternalOutput").ap()

    skind = "ExternalOutput" if taps else "Internal"
    a_scr = nc.dram_tensor("a_scr", [n_tok, I], F16, kind=skind).ap()
    p0_scr = nc.dram_tensor("p0_scr", [n_tok, I], F16, kind=skind).ap()
    p_scr = nc.dram_tensor("p_scr", [n_tok, I], F16, kind=skind).ap()
    r_scr = nc.dram_tensor("r_scr", [n_tok, D], F32, kind=skind).ap()
    h_scr = nc.dram_tensor("h_scr", [n_tok, D], F32, kind=skind).ap()
    h_f16 = nc.dram_tensor("h_f16", [n_tok, D], F16, kind=skind).ap()
    y_scr = nc.dram_tensor("y_scr", [n_tok, D], F32, kind=skind).ap()
    wd2 = nc.dram_tensor("wd2", [8, NOCT, 128, 8, 512], F16, kind="Internal").ap()
    wb2 = nc.dram_tensor("wb2", [8, 128, NJ, 512], F16, kind="Internal").ap()
    bias_f16 = nc.dram_tensor("bias_f16", [1, D], F16, kind="Internal").ap()
    if taps:
        tsel = nc.dram_tensor("tsel", [n_tok, 96], F32, kind="ExternalOutput").ap()
        dbg4 = nc.dram_tensor("dbg4", [n_tok // 512, 128, 1152], F32,
                              kind="ExternalOutput").ap()
        tsel2 = nc.dram_tensor("tsel2", [n_tok, 96], F32,
                               kind="ExternalOutput").ap()

    SB_BASE = 20480

    def A(name, shape, dtype, offset):
        return nc.alloc_sbuf_tensor_at(name, shape, dtype, offset=offset + SB_BASE)

    ident32 = A("ident32", [128, 128], F32, offset=0).ap()
    state = A("state", [128, 256], F32, offset=512).ap()
    sml = A("sml", [128, 256], F32, offset=1536).ap()
    P0 = 2560
    xTb = A("xTb", [128, NJ * THALF], F16, offset=P0).ap()
    o = P0 + NJ * THALF * 2
    guB = A("guB", [128, 2 * NJ * 512], F16, offset=o).ap()
    o += 2 * NJ * 512 * 2
    sRing = A("sRing", [128, 2 * IC], F32, offset=o).ap()
    o += 2 * IC * 4
    aStrip = A("aStrip", [128, 2 * NTH * IC], F16, offset=o).ap()
    o += 2 * NTH * IC * 2
    p0Strip = A("p0Strip", [128, 2 * NTH * IC], F16, offset=o).ap()
    o += 2 * NTH * IC * 2
    xsc = A("xsc", [128, 2 * IC], F32, offset=o).ap()
    st1_end = o + 2 * IC * 4
    pTs = A("pTs", [128, NI * 512], F16, offset=P0).ap()
    o2 = P0 + NI * 512 * 2
    wdR = A("wdR", [128, 2 * 8 * 512], F16, offset=o2).ap()
    o2 += 2 * 8 * 512 * 2
    xCh = A("xCh", [128, 2 * 512], F32, offset=o2).ap(); o2 += 2 * 512 * 4
    rCh = A("rCh", [128, 2 * 512], F32, offset=o2).ap(); o2 += 2 * 512 * 4
    sqJ = A("sqJ", [128, 512], F16, offset=o2).ap(); o2 += 512 * 2
    rlCh = A("rlCh", [128, 2 * 512], F32, offset=o2).ap(); o2 += 2 * 512 * 4
    hCh = A("hCh", [128, 2 * 512], F32, offset=o2).ap(); o2 += 2 * 512 * 4
    wbB = A("wbB", [128, 2 * NJ * 512], F16, offset=P0).ap()
    o4 = P0 + 2 * NJ * 512 * 2
    hTb = A("hTb", [128, NJ * 512], F16, offset=o4).ap()
    o4 += NJ * 512 * 2
    assert o4 <= P0 + NI * 512 * 2 + 2 * 8 * 512 * 2, o4
    o4 = o2
    gwB = A("gwB", [128, NJ * 128], F16, offset=o4).ap(); o4 += NJ * 128 * 2
    laB = A("laB", [128, NJ * 128], F16, offset=o4).ap(); o4 += NJ * 128 * 2
    lbB = A("lbB", [128, D], F16, offset=o4).ap(); o4 += D * 2
    biasOC = A("biasOC", [128, 2 * 512], F16, offset=o4).ap(); o4 += 2 * 512 * 2
    expB = A("expB", [8, 128], F16, offset=o4).ap(); o4 += 128 * 2
    onesB = A("onesB", [128, 128], F16, offset=o4).ap(); o4 += 128 * 2
    glT = A("glT", [8, 512], F32, offset=o4).ap(); o4 += 512 * 4
    glog = A("glog", [128, 32], F32, offset=o4).ap(); o4 += 32 * 4
    wful = A("wful", [128, 32], F32, offset=o4).ap(); o4 += 32 * 4
    wfT = A("wfT", [8, 512], F16, offset=o4).ap(); o4 += 512 * 2
    lowS = A("lowS", [128, 512], F32, offset=o4).ap(); o4 += 512 * 4
    lsT = A("lsT", [128, 512], F16, offset=o4).ap(); o4 += 512 * 2
    hRes, yCh, ylCh, yoCh, sqJ4 = xCh, rCh, rlCh, hCh, sqJ
    SELOFF = max(st1_end, o4)
    aT = A("aT", [128, I], F16, offset=SELOFF).ap()
    wrk = A("wrk", [128, I], F16, offset=SELOFF + I * 2).ap()
    SEL_END = SELOFF + 2 * I * 2
    assert SEL_END <= 208767, SEL_END

    S_LOHI, S_SLOHI, S_TH, S_CNT, S_MV, S_T1, S_T2 = 0, 2, 4, 5, 6, 10, 12
    S_E, S_DEN, S_N2 = 14, 15, 16
    S_SUM, S_SQ = 18, 50
    S_MU, S_RS, S_BI = 82, 86, 90
    S_M1, S_M2, S_T3 = 94, 98, 102
    S_BTH, S_BERR, S_BT1, S_BT2 = 104, 105, 106, 107
    S_MU_O, S_RS_O, S_BI_O = 108, 112, 116
    S_MU_P = (82, 108)
    S_RS_P = (86, 112)
    S_BI_P = (90, 116)
    S_MU2_P = (120, 132)
    S_RS2_P = (124, 136)
    S_BI2_P = (128, 140)

    pk = [nc.alloc_psum_tensor(f"bank{i}", [128, 512], F32).ap() for i in range(8)]

    em = Em(nc, rec, final)

    with nc.Block() as block:
        def g_prelude(g):
            ins = g.dma_start(out=ident32[:], in_=ident_in)
            em.inc(ins, "LDW", 16)
            em.mark("LDW", "ident")

        block.gpsimd(g_prelude)

        def sel_ld_a(q, t):
            half_needed = 0 if (t + 1) * 128 <= THALF else 1
            em.wait(q, "ST", ("s1st", half_needed, NCH - 1))
            if t >= 1:
                em.wait(q, "EVV", ("selmask", t - 1))
            ins = q.dma_start(out=aT[:], in_=a_scr[t * 128:(t + 1) * 128, :])
            em.inc(ins, "SEL", 16)
            em.mark("SEL", ("sella", t))

        def sel_ld_p0(q, t):
            em.wait(q, "EVV", ("selprobed", t))
            if t >= 1:
                em.wait(q, "SEL", ("selstore", t - 1))
            ins = q.dma_start(out=wrk[:], in_=p0_scr[t * 128:(t + 1) * 128, :])
            em.inc(ins, "SEL", 16)
            em.mark("SEL", ("selp0", t))

        def sel_store(q, t):
            em.wait(q, "EVV", ("selmask", t))
            ins = q.dma_start(out=p_scr[t * 128:(t + 1) * 128, :], in_=wrk[:])
            em.inc(ins, "SEL", 16)
            em.mark("SEL", ("selstore", t))
            if taps:
                ins = q.dma_start(out=tsel[t * 128:(t + 1) * 128, :],
                                  in_=state[:, 160:256])
                em.inc(ins, "SEL", 16)
                ins = q.dma_start(out=tsel2[t * 128:(t + 1) * 128, :],
                                  in_=sml[:, 0:96])
                em.inc(ins, "SEL", 16)

        def sel_pieces_for_tile(t, sink):
            lohi = state[:, S_LOHI:S_LOHI + 2]
            slohi = state[:, S_SLOHI:S_SLOHI + 2]
            th = state[:, S_TH:S_TH + 1]
            cnt = state[:, S_CNT:S_CNT + 1]
            mvq = state[:, S_MV:S_MV + 4]
            t1 = state[:, S_T1:S_T1 + 2]
            t2 = state[:, S_T2:S_T2 + 2]
            e = state[:, S_E:S_E + 1]
            den = state[:, S_DEN:S_DEN + 1]
            n2 = state[:, S_N2:S_N2 + 1]

            def p_init(v, t=t):
                v.memset(lohi[:, 0:1], -1.0)
                v.memset(lohi[:, 1:2], 32.0)
                v.memset(slohi[:, 0:1], float(SUBW))
                v.memset(slohi[:, 1:2], 0.0)
                v.memset(th, 0.2 - ACENTER)
                v.memset(state[:, S_BTH:S_BTH + 1], 0.2 - ACENTER)
                v.memset(state[:, S_BERR:S_BERR + 1], 1e30)
                v.drain()

            sink.append(("init", t, p_init))

            for it in range(NPROBE):
                W = I if it >= NPS else SUBW
                K = KSEL if it >= NPS else KSUB

                def p_probe(v, t=t, it=it, W=W, K=K):
                    if it == 0:
                        em.wait(v, "SEL", ("sella", t))
                        if t >= 1:
                            em.wait(v, "SEL", ("selstore", t - 1))
                    v.tensor_scalar(wrk[:, 0:W], aT[:, 0:W], th, 0.0,
                                    ALU.is_ge, ALU.add, accum_out=cnt)
                    v.drain()
                    if it >= NPS:
                        bth = state[:, S_BTH:S_BTH + 1]
                        berr = state[:, S_BERR:S_BERR + 1]
                        bt1 = state[:, S_BT1:S_BT1 + 1]
                        bt2 = state[:, S_BT2:S_BT2 + 1]
                        v.tensor_single_scalar(bt1, cnt, K, ALU.subtract)
                        v.drain()
                        v.tensor_tensor(bt1, bt1, bt1, ALU.mult)
                        v.drain()
                        v.tensor_tensor(bt2, bt1, berr, ALU.is_lt)
                        v.drain()
                        v.tensor_tensor(berr, berr, bt1, ALU.min)
                        v.tensor_tensor(bt1, th, bth, ALU.subtract)
                        v.drain()
                        v.tensor_tensor(bt1, bt1, bt2, ALU.mult)
                        v.drain()
                        v.tensor_tensor(bth, bth, bt1, ALU.add)
                        v.drain()
                    v.tensor_single_scalar(mvq[:, 0:1], cnt, K, ALU.is_ge)
                    v.drain()
                    v.tensor_scalar(mvq[:, 1:2], mvq[:, 0:1], -1.0, 1.0,
                                    ALU.mult, ALU.add)
                    v.drain()
                    v.tensor_copy(mvq[:, 2:3], mvq[:, 1:2])
                    v.tensor_copy(mvq[:, 3:4], mvq[:, 0:1])
                    v.drain()
                    v.tensor_scalar(t1, lohi, th, None, ALU.subtract)
                    v.drain()
                    v.tensor_tensor(t2, mvq[:, 0:2], t1, ALU.mult)
                    v.drain()
                    v.tensor_tensor(lohi, lohi, t2, ALU.subtract)
                    v.tensor_scalar(t1, slohi, cnt, None, ALU.subtract)
                    v.drain()
                    v.tensor_tensor(t2, mvq[:, 0:2], t1, ALU.mult)
                    v.drain()
                    v.tensor_tensor(slohi, slohi, t2, ALU.subtract)
                    v.drain()
                    v.tensor_single_scalar(t1, slohi, K, ALU.subtract)
                    v.drain()
                    v.tensor_tensor(t1, t1, mvq[:, 2:4], ALU.mult)
                    v.drain()
                    v.tensor_single_scalar(t1, t1, 0.5, ALU.mult)
                    v.drain()
                    v.tensor_tensor(slohi, slohi, t1, ALU.subtract)
                    v.drain()
                    if it == NPS - 1:
                        v.tensor_single_scalar(lohi[:, 0:1], th, -0.008, ALU.add)
                        v.tensor_single_scalar(lohi[:, 1:2], th, 0.008, ALU.add)
                        v.memset(slohi[:, 0:1], KSEL + 300.0)
                        v.memset(slohi[:, 1:2], KSEL - 300.0)
                        v.drain()
                    if it == 0:
                        v.memset(th, 0.45 - ACENTER)
                        v.drain()
                    elif it < NPROBE - 1:
                        v.tensor_tensor(e, lohi[:, 1:2], lohi[:, 0:1], ALU.subtract)
                        v.tensor_tensor(den, slohi[:, 0:1], slohi[:, 1:2],
                                        ALU.subtract)
                        v.drain()
                        v.reciprocal(den, den)
                        v.drain()
                        v.tensor_tensor(e, e, den, ALU.mult)
                        v.drain()
                        kk = KSEL if it >= NPS - 1 else KSUB
                        v.tensor_single_scalar(n2, slohi[:, 0:1], kk, ALU.subtract)
                        v.drain()
                        v.tensor_tensor(e, e, n2, ALU.mult)
                        v.drain()
                        v.tensor_tensor(th, lohi[:, 0:1], e, ALU.add)
                        v.drain()
                    if it == NPROBE - 1:
                        ins = v.tensor_copy(state[:, S_T3:S_T3 + 1],
                                            state[:, S_BTH:S_BTH + 1])
                        em.inc(ins, "EVV", 1)
                        em.mark("EVV", ("selprobed", t))

                sink.append(("probe", t, p_probe))

            def p_mask(v, t=t):
                em.wait(v, "SEL", ("selp0", t))
                if taps:
                    c6 = 160 + 6 * (t % 16)
                    v.tensor_copy(state[:, c6:c6 + 4], state[:, 0:4])
                    v.tensor_copy(state[:, c6 + 4:c6 + 5],
                                  state[:, S_BTH:S_BTH + 1])
                    v.tensor_copy(state[:, c6 + 5:c6 + 6],
                                  state[:, S_BERR:S_BERR + 1])
                    v.tensor_copy(sml[:, 0:48], aT[:, 64:112])
                    v.tensor_copy(sml[:, 48:72], wrk[:, 64:88])
                    v.drain()
                v.scalar_tensor_tensor(wrk[:], aT[:],
                                       state[:, S_BTH:S_BTH + 1], wrk[:],
                                       ALU.is_ge, ALU.mult)
                if taps:
                    v.drain()
                    v.tensor_copy(sml[:, 72:96], wrk[:, 64:88])
                ins = v.tensor_copy(state[:, S_T3 + 1:S_T3 + 2],
                                    state[:, S_BTH:S_BTH + 1])
                em.inc(ins, "EVV", 1)
                em.mark("EVV", ("selmask", t))

            sink.append(("mask", t, p_mask))

        NBUF = 2

        prep_jobs = []
        for ds_ in range(8):
            for op_ in range(NOCT):
                i0_, i1_ = op_ * 8, min(op_ * 8 + 8, NI)

                def pj_wd(g, ds_=ds_, op_=op_, i0_=i0_, i1_=i1_):
                    ins = g.dma_start(
                        out=wd2[ds_, op_, :, 0:i1_ - i0_, :],
                        in_=wdT[i0_ * 128:i1_ * 128,
                                ds_ * 512:(ds_ + 1) * 512].rearrange(
                                    "(b p) d -> p b d", p=128))
                    em.inc(ins, "LDW", 16)

                prep_jobs.append(pj_wd)
        for oc_ in range(8):

            def pj_wb(g, oc_=oc_):
                ins = g.dma_start(
                    out=wb2[oc_],
                    in_=wbT[:, oc_ * 512:(oc_ + 1) * 512].rearrange(
                        "(j p) w -> p j w", p=128))
                em.inc(ins, "LDW", 16)

            prep_jobs.append(pj_wb)

        def pj_bias(g):
            ins = g.dma_start(out=bias_f16, in_=bias_in)
            em.inc(ins, "LDW", 16)

        prep_jobs.append(pj_bias)
        assert len(prep_jobs) <= 3 * (NCH - NBUF), len(prep_jobs)

        def guv(b):
            return guB[:, b * NJ * 512:(b + 1) * NJ * 512].rearrange(
                "p (j w) -> p j w", j=NJ)

        def st1_load_chunk(g, half, c):
            gl = half * NCH + c
            if gl >= NBUF:
                ph, pc = divmod(gl - NBUF, NCH)
                em.wait(g, "MM", ("s1mm", ph, pc, NTH - 1))
            v = guv(gl % NBUF)
            ins = g.dma_start(
                out=v[:, :, 0:IC],
                in_=wgT[:, c * IC:(c + 1) * IC].rearrange("(j p) i -> p j i", p=128))
            em.inc(ins, "LDW", 16)
            ins = g.dma_start(
                out=v[:, :, IC:2 * IC],
                in_=wuT[:, c * IC:(c + 1) * IC].rearrange("(j p) i -> p j i", p=128))
            em.inc(ins, "LDW", 16)
            em.mark("LDW", ("s1ld", half, c))

        def emit_st1(half, sel_pieces, sync_sched):
            def g_loads(g):
                if half == 1:
                    em.wait(g, "MM", ("s1mm", 0, NCH - 1, NTH - 1))
                src = xT_in[:, half * THALF:(half + 1) * THALF].rearrange(
                    "(j p) t -> p j t", p=128)
                ins = g.dma_start(out=xTb[:].rearrange("p (j t) -> p j t", j=NJ),
                                  in_=src)
                em.inc(ins, "LDW", 16)
                em.mark("LDW", ("s1x", half))
                for c in range(NCH):
                    st1_load_chunk(g, half, c)
                    if half == 0 and c >= NBUF:
                        for _ in range(3):
                            if prep_jobs:
                                prep_jobs.pop(0)(g)
                if half == 0:
                    assert not prep_jobs, len(prep_jobs)
                    em.mark("LDW", "preconv")

            block.gpsimd(g_loads)

            def t_mms(t):
                em.wait(t, "LDW", ("s1x", half))
                for c in range(NCH):
                    v = guv((half * NCH + c) % NBUF)
                    em.wait(t, "LDW", ("s1ld", half, c))
                    for tt in range(NTH):
                        bank = pk[tt % 2]
                        gl = (half * NCH + c) * NTH + tt
                        if gl >= 2:
                            pg = gl - 2
                            ph, pr = divmod(pg, NCH * NTH)
                            pc, ptt = divmod(pr, NTH)
                            em.wait(t, "EVV", ("s1ev", ph, pc, ptt))
                        for j in range(NJ):
                            ins = t.matmul(
                                bank[:, 0:512],
                                xTb[:, j * THALF + tt * 128:
                                    j * THALF + (tt + 1) * 128],
                                v[:, j, :], start=(j == 0), stop=(j == NJ - 1))
                        em.inc(ins, "MM", 1)
                        em.mark("MM", ("s1mm", half, c, tt))

            block.tensor(t_mms)

            def s_silu(s):
                for c in range(NCH):
                    par = (half * NCH + c) % 2
                    for tt in range(NTH):
                        bank = pk[tt % 2]
                        em.wait(s, "MM", ("s1mm", half, c, tt))
                        gl = (half * NCH + c) * NTH + tt
                        if gl >= 2:
                            pg = gl - 2
                            ph, pr = divmod(pg, NCH * NTH)
                            pc, ptt = divmod(pr, NTH)
                            em.wait(s, "EVV", ("s1ev", ph, pc, ptt))
                        sr = sRing[:, (tt % 2) * IC:(tt % 2 + 1) * IC]
                        ins = s.activation(sr, bank[:, 0:IC],
                                           AF.Identity if os.environ.get('ZTAP') else AF.Silu)
                        em.inc(ins, "EVA", 1)
                        em.mark("EVA", ("s1silu", half, c, tt))

            block.scalar(s_silu)

            def v_evac(v):
                for c in range(NCH):
                    par = (half * NCH + c) % 2
                    for tt in range(NTH):
                        bank = pk[tt % 2]
                        em.wait(v, "EVA", ("s1silu", half, c, tt))
                        gl = half * NCH + c
                        if gl >= 2:
                            ph, pc = divmod(gl - 2, NCH)
                            em.wait(v, "ST", ("s1st", ph, pc))
                        sr = sRing[:, (tt % 2) * IC:(tt % 2 + 1) * IC]
                        v.tensor_tensor(
                            p0Strip[:, (par * NTH + tt) * IC:
                                    (par * NTH + tt + 1) * IC],
                            sr, bank[:, IC:2 * IC], ALU.mult)
                        v.scalar_tensor_tensor(xsc[:, (tt % 2) * IC:
                                                   (tt % 2 + 1) * IC],
                                               sr, -1.0, sr, ALU.mult, ALU.max)
                        v.drain()
                        ins = v.tensor_single_scalar(
                            aStrip[:, (par * NTH + tt) * IC:
                                   (par * NTH + tt + 1) * IC],
                            xsc[:, (tt % 2) * IC:(tt % 2 + 1) * IC],
                            ACENTER, ALU.subtract)
                        em.inc(ins, "EVV", 1)
                        em.mark("EVV", ("s1ev", half, c, tt))
                        if half == 1 and sel_pieces and tt % 2 == 0:
                            kind, pt, fn = sel_pieces.pop(0)
                            fn(v)
                            cc = max(0, c - 1)
                            if kind == "probe":
                                pieces_left = [p for p in sel_pieces
                                               if p[1] == pt and p[0] == "probe"]
                                if not pieces_left:
                                    sync_sched.setdefault(cc, []).append(
                                        lambda q, pt=pt: sel_ld_p0(q, pt))
                                    p0_done.add(pt)
                            if kind == "mask":
                                sync_sched.setdefault(cc, []).append(
                                    lambda q, pt=pt: sel_store(q, pt))
                                st_done.add(pt)
                                if pt + 1 in early_tiles:
                                    sync_sched.setdefault(cc, []).append(
                                        lambda q, pt=pt: sel_ld_a(q, pt + 1))
                                    lda_done.add(pt + 1)

            block.vector(v_evac)

            def sy_store(sy):
                for c in range(NCH):
                    par = (half * NCH + c) % 2
                    em.wait(sy, "EVV", ("s1ev", half, c, NTH - 1))
                    r0, r1 = half * THALF, (half + 1) * THALF
                    ins = sy.dma_start(
                        out=a_scr[r0:r1, c * IC:(c + 1) * IC].rearrange(
                            "(tt p) i -> p tt i", p=128),
                        in_=aStrip[:, par * NTH * IC:(par + 1) * NTH * IC].rearrange(
                            "p (tt i) -> p tt i", tt=NTH))
                    em.inc(ins, "ST", 16)
                    ins = sy.dma_start(
                        out=p0_scr[r0:r1, c * IC:(c + 1) * IC].rearrange(
                            "(tt p) i -> p tt i", p=128),
                        in_=p0Strip[:, par * NTH * IC:(par + 1) * NTH * IC].rearrange(
                            "p (tt i) -> p tt i", tt=NTH))
                    em.inc(ins, "ST", 16)
                    em.mark("ST", ("s1st", half, c))
                    for fn in sync_sched.get(c, []):
                        fn(sy)

            block.sync(sy_store)

        early_tiles = [t for t in range(min(NT, 8)) if (t + 1) * 128 <= THALF]
        if stages < 2 or os.environ.get("SELMODE", "") == "late":
            early_tiles = []
        late_tiles = [t for t in range(min(NT, 8)) if t not in early_tiles]
        if stages < 2 or os.environ.get("SELMODE", "") == "early":
            late_tiles = []
        early_pieces = []
        for t in early_tiles:
            sel_pieces_for_tile(t, early_pieces)
        sched = {}
        sched0 = {}
        lda_done, p0_done, st_done = set(), set(), set()
        if early_tiles:
            t0 = early_tiles[0]
            sched0.setdefault(NCH - 1, []).append(
                lambda q, t0=t0: sel_ld_a(q, t0))
            lda_done.add(t0)

        emit_st1(0, [], sched0)
        emit_st1(1, early_pieces, sched)
        if early_pieces:
            rest = list(early_pieces)
            early_pieces.clear()

            def v_left(v, ps=rest):
                for _, _, p in ps:
                    p(v)

            block.vector(v_left)
        for t in early_tiles:
            if t not in lda_done:
                block.sync(lambda q, t=t: sel_ld_a(q, t))
                lda_done.add(t)
            if t not in p0_done:
                block.sync(lambda q, t=t: sel_ld_p0(q, t))
                p0_done.add(t)
            if t not in st_done:
                block.sync(lambda q, t=t: sel_store(q, t))
                st_done.add(t)
        for t in late_tiles:
            block.sync(lambda q, t=t: sel_ld_a(q, t))
            lp = []
            sel_pieces_for_tile(t, lp)

            def v_lt(v, lp=lp):
                for _, _, p in lp:
                    p(v)

            block.vector(v_lt)
            block.sync(lambda q, t=t: sel_ld_p0(q, t))
            block.sync(lambda q, t=t: sel_store(q, t))

        def sc_tp_body(sc, kk):
            rr = kk * 512
            for t in range(4):
                em.wait(sc, "SEL", ("selstore", 4 * kk + t))
            for i in range(NI):
                if kk >= 1:
                    em.wait(sc, "MM", ("mmoct", kk - 1, 7, min(i // 8, NOCT - 1)))
                ins = sc.dma_start_transpose(
                    pTs[:, i * 512:(i + 1) * 512],
                    p_scr[rr:rr + 512, i * 128:(i + 1) * 128])
                em.inc(ins, "LDT", 16)
                if i % 8 == 7:
                    em.mark("LDT", ("tpo", kk, i // 8))
            ins = sc.dma_start_transpose(
                pTs[:, (NI - 1) * 512:NI * 512],
                p_scr[rr:rr + 512, (NI - 1) * 128:NI * 128])
            em.inc(ins, "LDT", 16)
            em.mark("LDT", ("tpo", kk, (NI - 1) // 8))
            em.mark("LDT", ("tp", kk))

        def ln_io_g(g, kk):
            rr = kk * 512

            def hstore(pidx):
                ptt, pds = pidx // 8, pidx % 8
                em.wait(g, "EVA", ("app", kk, pidx))
                ins = g.dma_start(
                    out=h_scr[rr + ptt * 128:rr + ptt * 128 + 128,
                              pds * 512:(pds + 1) * 512],
                    in_=hCh[:, (pidx % 2) * 512:(pidx % 2 + 1) * 512])
                em.inc(ins, "HS", 16)
                em.mark("HS", ("hs", kk, pidx))
                ins = g.dma_start(
                    out=h_f16[rr + ptt * 128:rr + ptt * 128 + 128,
                              pds * 512:(pds + 1) * 512],
                    in_=hCh[:, (pidx % 2) * 512:(pidx % 2 + 1) * 512])
                em.inc(ins, "STG", 16)
                em.mark("STG", ("hc", kk, pidx))

            for tt in range(4):
                for ds in range(8):
                    idx = tt * 8 + ds
                    em.wait(g, "ST", ("rst", kk, ds, tt))
                    if idx >= 2:
                        em.wait(g, "EVA", ("app", kk, idx - 2))
                    elif kk >= 1:
                        em.wait(g, "EVA", ("app", kk - 1, 30 + idx))
                    ins = g.dma_start(
                        out=rlCh[:, (idx % 2) * 512:(idx % 2 + 1) * 512],
                        in_=r_scr[rr + tt * 128:rr + tt * 128 + 128,
                                  ds * 512:(ds + 1) * 512])
                    em.inc(ins, "RL", 16)
                    em.mark("RL", ("rl", kk, idx))
                    if idx >= 1:
                        hstore(idx - 1)
            hstore(31)

        for k in range(NSLAB if stages >= 3 else 0):
            r0 = k * 512

            pend = []
            pend_sched = {}
            sel_cycle_tiles = []
            if k + 2 < NSLAB:
                sel_cycle_tiles = list(range(4 * (k + 2), 4 * (k + 3)))
                for t in sel_cycle_tiles:
                    sel_pieces_for_tile(t, pend)
                t0 = sel_cycle_tiles[0]
                pend_sched.setdefault(-1, []).append(
                    lambda q, t0=t0: sel_ld_a(q, t0))

            if k == 0:
                block.scalar(lambda sc: sc_tp_body(sc, 0))

            S2LVL = int(os.environ.get("S2LVL", "4"))

            def sy_main(sy, k=k):
                if S2LVL < 2:
                    return
                for fn in pend_sched.get(-1, []):
                    fn(sy)
                for ds in range(8):
                    for o_ in range(NOCT):
                        i0, i1 = o_ * 8, min(o_ * 8 + 8, NI)
                        idx = ds * NOCT + o_
                        if idx >= 2:
                            em.wait(sy, "MM", ("mmoct", k, (idx - 2) // NOCT,
                                               (idx - 2) % NOCT))
                        elif k >= 1:
                            em.wait(sy, "MM", ("mmoct", k - 1, 7, NOCT - 1))
                        elif idx == 0:
                            em.wait(sy, "LDW", "preconv")
                        src = wd2[ds, o_, :, 0:i1 - i0, :]
                        dst = wdR[:, (idx % 2) * 8 * 512:
                                  (idx % 2) * 8 * 512 + (i1 - i0) * 512].rearrange(
                                      "p (b w) -> p b w", b=i1 - i0)
                        ins = sy.dma_start(out=dst, in_=src)
                        em.inc(ins, "ST", 16)
                        em.mark("ST", ("wd", k, ds, o_))
                    if ds >= 1:
                        for tt in range(4):
                            em.wait(sy, "EVV", ("ev2", k, ds - 1, tt))
                            pidx = (ds - 1) * 4 + tt
                            ins = sy.dma_start(
                                out=r_scr[r0 + tt * 128:r0 + tt * 128 + 128,
                                          (ds - 1) * 512:ds * 512],
                                in_=rCh[:, (pidx % 2) * 512:(pidx % 2 + 1) * 512])
                            em.inc(ins, "ST", 16)
                            em.mark("ST", ("rst", k, ds - 1, tt))
                    for fn in (pend_sched.get(ds - 1, []) if ds >= 1 else []):
                        fn(sy)
                    for tt in range(4):
                        idx = ds * 4 + tt
                        if idx >= 2:
                            em.wait(sy, "EVV", ("ev2", k, (idx - 2) // 4,
                                                (idx - 2) % 4))
                        elif k >= 1:
                            em.wait(sy, "EVV", ("ev2", k - 1, 7, 2 + idx))
                        ins = sy.dma_start(
                            out=xCh[:, (idx % 2) * 512:(idx % 2 + 1) * 512],
                            in_=x_in[r0 + tt * 128:r0 + tt * 128 + 128,
                                     ds * 512:(ds + 1) * 512])
                        em.inc(ins, "ST", 16)
                        em.mark("ST", ("x2", k, ds, tt))
                for tt in range(4):
                    em.wait(sy, "EVV", ("ev2", k, 7, tt))
                    pidx = 7 * 4 + tt
                    ins = sy.dma_start(
                        out=r_scr[r0 + tt * 128:r0 + tt * 128 + 128, 7 * 512:8 * 512],
                        in_=rCh[:, (pidx % 2) * 512:(pidx % 2 + 1) * 512])
                    em.inc(ins, "ST", 16)
                    em.mark("ST", ("rst", k, 7, tt))
                for b in (7, 99):
                    for fn in pend_sched.get(b, []):
                        fn(sy)
                if os.environ.get("S2LN", "1") == "0":
                    return

            def t_2b(t, k=k):
                if S2LVL < 3:
                    return
                if k >= 1:
                    em.wait(t, "EVV", ("ev2", k - 1, 7, 3))
                for ds in range(8):
                    bset = pk[0:4] if ds % 2 == 0 else pk[4:8]
                    if ds >= 2:
                        em.wait(t, "EVV", ("ev2", k, ds - 2, 3))
                    for o_ in range(NOCT):
                        i0, i1 = o_ * 8, min(o_ * 8 + 8, NI)
                        idx = ds * NOCT + o_
                        if ds == 0 and o_ == 0:
                            em.wait(t, "LDT", ("tp", k))
                        em.wait(t, "ST", ("wd", k, ds, min(o_ + 1, NOCT - 1)))
                        for i in range(i0, i1):
                            wv = wdR[:, (idx % 2) * 8 * 512 + (i - i0) * 512:
                                     (idx % 2) * 8 * 512 + (i - i0 + 1) * 512]
                            for tt in range(4):
                                ins = t.matmul(
                                    bset[tt][:, 0:512],
                                    pTs[:, i * 512 + tt * 128:
                                        i * 512 + (tt + 1) * 128],
                                    wv, start=(i == 0), stop=(i == NI - 1))
                        em.inc(ins, "MM", 1)
                        em.mark("MM", ("mmoct", k, ds, o_))

            block.tensor(t_2b)

            def v_2b(v, k=k, pend=pend):
                if S2LVL < 4:
                    while pend:
                        pend.pop(0)[2](v)
                    return
                for ds in range(8):
                    bset = pk[0:4] if ds % 2 == 0 else pk[4:8]
                    for tt in range(4):
                        idx = ds * 4 + tt
                        em.wait(v, "MM", ("mmoct", k, ds, NOCT - 1))
                        em.wait(v, "ST", ("x2", k, ds, min(tt + 1, 3)))
                        if idx >= 2:
                            em.wait(v, "ST", ("rst", k, (idx - 2) // 4,
                                              (idx - 2) % 4))
                        elif k >= 1:
                            em.wait(v, "ST", ("rst", k - 1, 7, 2 + idx))
                        cp = idx % 2
                        acc1 = (state[:, S_SUM + tt * 8 + ds:
                                      S_SUM + tt * 8 + ds + 1]
                                if os.environ.get("S2ACC", "1") == "1" else None)
                        ins = v.scalar_tensor_tensor(
                            rCh[:, cp * 512:(cp + 1) * 512],
                            bset[tt][:, 0:512], 1.0,
                            xCh[:, cp * 512:(cp + 1) * 512],
                            ALU.mult, ALU.add,
                            accum_out=acc1)
                        if os.environ.get("S2SQ", "1") == "1":
                            acc2 = (state[:, S_SQ + tt * 8 + ds:
                                          S_SQ + tt * 8 + ds + 1]
                                    if os.environ.get("S2ACC", "1") == "1" else None)
                            ins = v.scalar_tensor_tensor(
                                sqJ[:], rCh[:, cp * 512:(cp + 1) * 512], 1.0,
                                rCh[:, cp * 512:(cp + 1) * 512],
                                ALU.mult, ALU.mult,
                                accum_out=acc2)
                        em.inc(ins, "EVV", 1)
                        em.mark("EVV", ("ev2", k, ds, tt))
                    for _ in range(7):
                        if pend:
                            kind, pt, fn = pend.pop(0)
                            fn(v)
                            if kind == "probe" and not any(
                                    p[1] == pt and p[0] == "probe" for p in pend):
                                pend_sched.setdefault(ds, []).append(
                                    lambda q, pt=pt: sel_ld_p0(q, pt))
                            if kind == "mask":
                                pend_sched.setdefault(ds, []).append(
                                    lambda q, pt=pt: sel_store(q, pt))
                                if pt + 1 in sel_cycle_tiles:
                                    pend_sched.setdefault(ds, []).append(
                                        lambda q, pt=pt: sel_ld_a(q, pt + 1))
                while pend:
                    kind, pt, fn = pend.pop(0)
                    fn(v)
                    if kind == "probe" and not any(
                            p[1] == pt and p[0] == "probe" for p in pend):
                        pend_sched.setdefault(99, []).append(
                            lambda q, pt=pt: sel_ld_p0(q, pt))
                    if kind == "mask":
                        pend_sched.setdefault(99, []).append(
                            lambda q, pt=pt: sel_store(q, pt))
                        if pt + 1 in sel_cycle_tiles:
                            pend_sched.setdefault(99, []).append(
                                lambda q, pt=pt: sel_ld_a(q, pt + 1))
                if os.environ.get("S2LN", "1") == "0":
                    return
                if k >= 2:
                    em.wait(v, "EVA", ("app", k - 2, 31))
                v.drain()
                MUB = S_MU if k % 2 == 0 else S_MU_O
                RSB = S_RS if k % 2 == 0 else S_RS_O
                for tt in range(4):
                    mu = state[:, MUB + tt:MUB + tt + 1]
                    va = state[:, S_T1:S_T1 + 1]
                    v.tensor_reduce(mu, state[:, S_SUM + tt * 8:S_SUM + tt * 8 + 8],
                                    AX.X, ALU.add)
                    v.tensor_reduce(va, state[:, S_SQ + tt * 8:S_SQ + tt * 8 + 8],
                                    AX.X, ALU.add)
                    v.drain()
                    v.tensor_single_scalar(mu, mu, 1.0 / D, ALU.mult)
                    v.tensor_single_scalar(va, va, 1.0 / D, ALU.mult)
                    v.drain()
                    v.tensor_tensor(state[:, S_T2:S_T2 + 1], mu, mu, ALU.mult)
                    v.drain()
                    v.tensor_tensor(va, va, state[:, S_T2:S_T2 + 1], ALU.subtract)
                    v.drain()
                    ins = v.tensor_single_scalar(
                        state[:, RSB + tt:RSB + tt + 1], va, LN_EPS, ALU.add)
                    em.inc(ins, "EVV", 1)
                    em.mark("EVV", ("var", k, tt))

            block.vector(v_2b)
            block.sync(sy_main)

            if k + 1 < NSLAB:
                block.scalar(lambda sc, kk=k + 1: sc_tp_body(sc, kk))

            def s_ln(s, k=k):
                RSB = S_RS if k % 2 == 0 else S_RS_O
                for tt in range(4):
                    em.wait(s, "EVV", ("var", k, tt))
                    ins = s.activation(state[:, RSB + tt:RSB + tt + 1],
                                       state[:, RSB + tt:RSB + tt + 1], AF.Sqrt)
                    em.inc(ins, "EVA", 1)
                    em.mark("EVA", ("sqrt", k, tt))

            if os.environ.get('S2LN', '1') == '1':
                block.scalar(s_ln)

            def v_fin(v, k=k):
                MUB = S_MU if k % 2 == 0 else S_MU_O
                RSB = S_RS if k % 2 == 0 else S_RS_O
                BIB = S_BI if k % 2 == 0 else S_BI_O
                for tt in range(4):
                    em.wait(v, "EVA", ("sqrt", k, tt))
                    rs = state[:, RSB + tt:RSB + tt + 1]
                    v.reciprocal(rs, rs)
                    v.drain()
                    ins = v.scalar_tensor_tensor(
                        state[:, BIB + tt:BIB + tt + 1],
                        state[:, MUB + tt:MUB + tt + 1],
                        -1.0, rs, ALU.mult, ALU.mult)
                    em.inc(ins, "EVV", 1)
                    em.mark("EVV", ("fin", k, tt))

            if os.environ.get('S2LN', '1') == '1':
                block.vector(v_fin)

            def s_apply(s, k=k):
                RSB = S_RS if k % 2 == 0 else S_RS_O
                BIB = S_BI if k % 2 == 0 else S_BI_O
                for tt in range(4):
                    em.wait(s, "EVV", ("fin", k, tt))
                    for ds in range(8):
                        idx = tt * 8 + ds
                        em.wait(s, "RL", ("rl", k, idx))
                        if idx >= 2:
                            em.wait(s, "HS", ("hs", k, idx - 2))
                            em.wait(s, "STG", ("hc", k, idx - 2))
                        elif k >= 1:
                            em.wait(s, "HS", ("hs", k - 1, 30 + idx))
                            em.wait(s, "STG", ("hc", k - 1, 30 + idx))
                        ins = s.activation(
                            hCh[:, (idx % 2) * 512:(idx % 2 + 1) * 512],
                            rlCh[:, (idx % 2) * 512:(idx % 2 + 1) * 512],
                            AF.Identity,
                            bias=state[:, BIB + tt:BIB + tt + 1],
                            scale=state[:, RSB + tt:RSB + tt + 1])
                        em.inc(ins, "EVA", 1)
                        em.mark("EVA", ("app", k, idx))

            if os.environ.get('S2LN', '1') == '1':
                block.scalar(s_apply)

            if os.environ.get('S2LN', '1') == '1' and k >= 1:
                block.gpsimd(lambda g, kk=k - 1: ln_io_g(g, kk))

        if stages >= 3 and NSLAB >= 1 and os.environ.get("S2LN", "1") == "1" \
                and int(os.environ.get("S2LVL", "4")) >= 2:
            block.gpsimd(lambda g: ln_io_g(g, NSLAB - 1))

        def g_smalls(g):
            em.wait(g, "MM", ("mmoct", NSLAB - 1, 7, NOCT - 1))
            em.inc(g.dma_start(out=expB[:], in_=expm_in), "LDW", 16)
            em.inc(g.dma_start(out=onesB[:], in_=ones_in), "LDW", 16)
            em.inc(g.dma_start(out=gwB[:], in_=gwP), "LDW", 16)
            em.inc(g.dma_start(out=laB[:].rearrange("p (j r) -> p j r", j=NJ),
                               in_=laT.rearrange("(j p) r -> p j r", p=128)),
                   "LDW", 16)
            em.inc(g.dma_start(out=lbB[:], in_=lbT), "LDW", 16)
            em.mark("LDW", "smalls")

        if stages >= 4:
            block.gpsimd(g_smalls)

        for k in range(NSLAB if stages >= 4 else 0):
            r0 = k * 512

            def sc_htp(sc, k=k):
                for idx in range(32):
                    em.wait(sc, "STG", ("hc", k, idx))
                if k >= 1:
                    em.wait(sc, "MM", ("mm4fin", k - 1))
                else:
                    em.wait(sc, "MM", ("mmoct", NSLAB - 1, 7, NOCT - 1))
                    em.wait(sc, "SEL", ("selstore", NT - 1))
                for j in range(NJ):
                    ins = sc.dma_start_transpose(
                        hTb[:, j * 512:(j + 1) * 512],
                        h_f16[r0:r0 + 512, j * 128:(j + 1) * 128])
                    em.inc(ins, "LDT", 16)
                em.mark("LDT", ("htp", k))

            block.scalar(sc_htp)

            def t_p1(t, k=k):
                em.wait(t, "LDW", "smalls")
                em.wait(t, "LDW", "ident")
                em.wait(t, "LDT", ("htp", k))
                if k >= 1:
                    em.wait(t, "EVV", ("ev4", k - 1, 7, 3))
                else:
                    em.wait(t, "EVV", ("ev2", NSLAB - 1, 7, 3))
                for j in range(NJ):
                    ins = t.matmul(pk[0][:, 0:512], gwB[:, j * 128:(j + 1) * 128],
                                   hTb[:, j * 512:(j + 1) * 512],
                                   start=(j == 0), stop=(j == NJ - 1))
                em.inc(ins, "MM", 1)
                em.mark("MM", ("mmgl", k))
                for j in range(NJ):
                    ins = t.matmul(pk[4][:, 0:512], laB[:, j * 128:(j + 1) * 128],
                                   hTb[:, j * 512:(j + 1) * 512],
                                   start=(j == 0), stop=(j == NJ - 1))
                em.inc(ins, "MM", 1)
                em.mark("MM", ("mmlow", k))

            block.tensor(t_p1)

            def s_p1(s, k=k):
                em.wait(s, "MM", ("mmgl", k))
                ins = s.copy(glT[:], pk[0][0:8, 0:512])
                em.inc(ins, "EVA", 1)
                em.mark("EVA", ("glev", k))

            block.scalar(s_p1)

            def t_p2(t, k=k):
                em.wait(t, "EVA", ("glev", k))
                for tt in range(4):
                    ins = t.transpose(pk[1][:, tt * 8:(tt + 1) * 8],
                                      glT[0:8, tt * 128:(tt + 1) * 128],
                                      ident32[0:8, 0:8])
                em.inc(ins, "MM", 1)
                em.mark("MM", ("mmglt", k))

            block.tensor(t_p2)

            def v_p1(v, k=k):
                if k == 0:
                    v.memset(biasOC[:], 0.0)
                    v.drain()
                em.wait(v, "MM", ("mmglt", k))
                v.tensor_copy(glog[:], pk[1][:, 0:32])
                v.drain()
                for tt in range(4):
                    g8 = glog[:, tt * 8:(tt + 1) * 8]
                    m1 = state[:, S_M1 + tt:S_M1 + tt + 1]
                    m2 = state[:, S_M2 + tt:S_M2 + tt + 1]
                    g2 = sml[:, tt * 8:tt * 8 + 8]
                    mk1 = sml[:, 64 + tt * 8:64 + tt * 8 + 8]
                    v.tensor_reduce(m1, g8, AX.X, ALU.max)
                    v.drain()
                    v.tensor_scalar(mk1, g8, m1, None, ALU.is_ge)
                    v.drain()
                    v.scalar_tensor_tensor(g2, mk1, -1e30, g8, ALU.mult, ALU.add)
                    v.drain()
                    v.tensor_reduce(m2, g2, AX.X, ALU.max)
                    v.drain()
                    ins = v.tensor_tensor(m2, m2, m1, ALU.subtract)
                    v.drain()
                em.inc(ins, "EVV", 1)
                em.mark("EVV", ("m12", k))

            block.vector(v_p1)

            def s_p2(s, k=k):
                em.wait(s, "EVV", ("m12", k))
                ins = s.activation(state[:, S_M2:S_M2 + 4], state[:, S_M2:S_M2 + 4],
                                   AF.Exp)
                em.inc(ins, "EVA", 1)
                em.mark("EVA", ("exp", k))

            block.scalar(s_p2)

            def v_p2(v, k=k):
                em.wait(v, "EVA", ("exp", k))
                for tt in range(4):
                    g8 = glog[:, tt * 8:(tt + 1) * 8]
                    g2 = sml[:, tt * 8:tt * 8 + 8]
                    mk1 = sml[:, 64 + tt * 8:64 + tt * 8 + 8]
                    mk2 = sml[:, 128 + tt * 8:128 + tt * 8 + 8]
                    ee = state[:, S_M2 + tt:S_M2 + tt + 1]
                    w1 = state[:, S_T1:S_T1 + 1]
                    w2 = state[:, S_T2:S_T2 + 1]
                    m2v = state[:, S_T3:S_T3 + 1]
                    v.tensor_single_scalar(w1, ee, 1.0, ALU.add)
                    v.drain()
                    v.reciprocal(w1, w1)
                    v.drain()
                    v.tensor_single_scalar(w1, w1, SCALING, ALU.mult)
                    v.drain()
                    v.tensor_single_scalar(w2, w1, -1.0, ALU.mult)
                    v.drain()
                    v.tensor_single_scalar(w2, w2, SCALING, ALU.add)
                    v.tensor_reduce(m2v, g2, AX.X, ALU.max)
                    v.drain()
                    v.tensor_scalar(mk2, g2, m2v, None, ALU.is_ge)
                    v.drain()
                    v.tensor_scalar(mk1, mk1, w1, None, ALU.mult)
                    v.tensor_scalar(mk2, mk2, w2, None, ALU.mult)
                    v.drain()
                    ins = v.tensor_tensor(wful[:, tt * 8:(tt + 1) * 8], mk1, mk2,
                                          ALU.add)
                    v.drain()
                em.inc(ins, "EVV", 1)
                em.mark("EVV", ("wful", k))

            block.vector(v_p2)

            def t_p3(t, k=k):
                em.wait(t, "EVV", ("wful", k))
                for tt in range(4):
                    ins = t.transpose(pk[2][0:8, tt * 128:(tt + 1) * 128],
                                      wful[:, tt * 8:(tt + 1) * 8], ident32[:])
                em.inc(ins, "MM", 1)
                em.mark("MM", ("mmwft", k))

            block.tensor(t_p3)

            def s_p3(s, k=k):
                em.wait(s, "MM", ("mmwft", k))
                ins = s.copy(wfT[:], pk[2][0:8, 0:512])
                em.inc(ins, "EVA", 1)
                em.mark("EVA", ("wftev", k))

            block.scalar(s_p3)

            def t_p4(t, k=k):
                em.wait(t, "EVA", ("wftev", k))
                ins = t.matmul(pk[3][:, 0:512], expB[:], wfT[:],
                               start=True, stop=True)
                em.inc(ins, "MM", 1)
                em.mark("MM", ("mmwex", k))

            block.tensor(t_p4)

            def v_p3(v, k=k):
                em.wait(v, "MM", ("mmlow", k))
                v.tensor_copy(lowS[:], pk[4][:, 0:512])
                v.drain()
                em.wait(v, "MM", ("mmwex", k))
                ins = v.tensor_tensor(lsT[:], lowS[:], pk[3][:, 0:512], ALU.mult)
                em.inc(ins, "EVV", 1)
                em.mark("EVV", ("lsT", k))

            block.vector(v_p3)

            if taps:
                def sy_dbg4(g, k=k):
                    em.wait(g, "EVV", ("lsT", k))
                    ins = g.dma_start(out=dbg4[k, :, 0:512], in_=lsT[:])
                    em.inc(ins, "STG", 16)
                    ins = g.dma_start(out=dbg4[k, :, 512:544], in_=glog[:])
                    em.inc(ins, "STG", 16)
                    ins = g.dma_start(out=dbg4[k, :, 544:576], in_=wful[:])
                    em.inc(ins, "STG", 16)
                    ins = g.dma_start(out=dbg4[k, :, 576:1088], in_=lowS[:])
                    em.inc(ins, "STG", 16)
                    ins = g.dma_start(out=dbg4[k, :, 1088:1120],
                                      in_=hTb[:, 0:32])
                    em.inc(ins, "STG", 16)
                    ins = g.dma_start(out=dbg4[k, :, 1120:1152],
                                      in_=laB[:, 2048:2080])
                    em.inc(ins, "STG", 16)

                block.gpsimd(sy_dbg4)

            def sy_4(sy, k=k):
                if os.environ.get("S4SER", "0") == "1":
                    em.wait(sy, "EVV", ("lsT", k))
                if k == 0:
                    em.wait(sy, "MM", ("mmoct", NSLAB - 1, 7, NOCT - 1))

                def wbload(oc):
                    if oc >= 2:
                        em.wait(sy, "MM", ("mm4", k, oc - 2, 3))
                    elif k >= 1:
                        em.wait(sy, "MM", ("mm4", k - 1, 6 + oc, 3))
                    src = wb2[oc]
                    dst = wbB[:, (oc % 2) * NJ * 512:
                              ((oc % 2) + 1) * NJ * 512].rearrange(
                                  "p (j w) -> p j w", j=NJ)
                    ins = sy.dma_start(out=dst, in_=src)
                    em.inc(ins, "ST", 16)
                    ins = sy.dma_start(
                        out=biasOC[0:1, (oc % 2) * 512:(oc % 2 + 1) * 512],
                        in_=bias_f16[0:1, oc * 512:(oc + 1) * 512])
                    em.inc(ins, "ST", 16)
                    em.mark("ST", ("wb", k, oc))

                wbload(0)
                for oc in range(8):
                    if oc + 1 < 8:
                        wbload(oc + 1)
                    if oc >= 1:
                        for tt in range(4):
                            em.wait(sy, "EVV", ("ev4", k, oc - 1, tt))
                            pidx = (oc - 1) * 4 + tt
                            ins = sy.dma_start(
                                out=y_scr[r0 + tt * 128:r0 + tt * 128 + 128,
                                          (oc - 1) * 512:oc * 512],
                                in_=yCh[:, (pidx % 2) * 512:(pidx % 2 + 1) * 512])
                            em.inc(ins, "ST", 16)
                            em.mark("ST", ("yst", k, oc - 1, tt))
                    for tt in range(4):
                        idx = oc * 4 + tt
                        if idx >= 2:
                            em.wait(sy, "EVV", ("ev4", k, (idx - 2) // 4,
                                                (idx - 2) % 4))
                        elif k == 0:
                            em.wait(sy, "EVV", ("ev2", NSLAB - 1, 7, 3))
                        else:
                            em.wait(sy, "EVV", ("ev4", k - 1, 7, 2 + idx))
                        em.wait(sy, "HS", ("hs", k, tt * 8 + oc))
                        ins = sy.dma_start(
                            out=hRes[:, (idx % 2) * 512:(idx % 2 + 1) * 512],
                            in_=h_scr[r0 + tt * 128:r0 + tt * 128 + 128,
                                      oc * 512:(oc + 1) * 512])
                        em.inc(ins, "ST", 16)
                        em.mark("ST", ("h4", k, oc, tt))
                for tt in range(4):
                    em.wait(sy, "EVV", ("ev4", k, 7, tt))
                    pidx = 7 * 4 + tt
                    ins = sy.dma_start(
                        out=y_scr[r0 + tt * 128:r0 + tt * 128 + 128,
                                  7 * 512:8 * 512],
                        in_=yCh[:, (pidx % 2) * 512:(pidx % 2 + 1) * 512])
                    em.inc(ins, "ST", 16)
                    em.mark("ST", ("yst", k, 7, tt))
                for tt in range(4):
                    for ds in range(8):
                        idx = tt * 8 + ds
                        em.wait(sy, "ST", ("yst", k, ds, tt))
                        if idx >= 2:
                            em.wait(sy, "EVA", ("app4", k, idx - 2))
                        elif k == 0:
                            em.wait(sy, "EVA", ("app", NSLAB - 1, 31))
                        else:
                            em.wait(sy, "EVA", ("app4", k - 1, 30 + idx))
                        ins = sy.dma_start(
                            out=ylCh[:, (idx % 2) * 512:(idx % 2 + 1) * 512],
                            in_=y_scr[r0 + tt * 128:r0 + tt * 128 + 128,
                                      ds * 512:(ds + 1) * 512])
                        em.inc(ins, "ST", 16)
                        em.mark("ST", ("yl", k, idx))
                        if idx >= 1:
                            pidx = idx - 1
                            ptt, pds = pidx // 8, pidx % 8
                            em.wait(sy, "EVA", ("app4", k, pidx))
                            ins = sy.dma_start(
                                out=y_out[r0 + ptt * 128:r0 + ptt * 128 + 128,
                                          pds * 512:(pds + 1) * 512],
                                in_=yoCh[:, (pidx % 2) * 512:(pidx % 2 + 1) * 512])
                            em.inc(ins, "ST", 16)
                            em.mark("ST", ("yo", k, pidx))
                em.wait(sy, "EVA", ("app4", k, 31))
                ins = sy.dma_start(
                    out=y_out[r0 + 3 * 128:r0 + 4 * 128, 7 * 512:8 * 512],
                    in_=yoCh[:, (31 % 2) * 512:(31 % 2 + 1) * 512])
                em.inc(ins, "ST", 16)
                em.mark("ST", ("yo", k, 31))

            block.sync(sy_4)

            def t_4(t, k=k):
                em.wait(t, "EVV", ("lsT", k))
                for oc in range(8):
                    bset = pk[0:4] if oc % 2 == 0 else pk[4:8]
                    em.wait(t, "ST", ("wb", k, min(oc + 1, 7)))
                    if oc >= 2:
                        em.wait(t, "EVV", ("ev4", k, oc - 2, 3))
                    wbv = wbB[:, (oc % 2) * NJ * 512:((oc % 2) + 1) * NJ * 512]
                    for tt in range(4):
                        for j in range(NJ):
                            t.matmul(bset[tt][:, 0:512],
                                     hTb[:, j * 512 + tt * 128:
                                         j * 512 + (tt + 1) * 128],
                                     wbv[:, j * 512:(j + 1) * 512],
                                     start=(j == 0), stop=False)
                        t.matmul(bset[tt][:, 0:512], onesB[:],
                                 biasOC[:, (oc % 2) * 512:(oc % 2 + 1) * 512],
                                 start=False, stop=False)
                        ins = t.matmul(bset[tt][:, 0:512],
                                       lsT[:, tt * 128:(tt + 1) * 128],
                                       lbB[:, oc * 512:(oc + 1) * 512],
                                       start=False, stop=True)
                        em.inc(ins, "MM", 1)
                        em.mark("MM", ("mm4", k, oc, tt))
                em.mark("MM", ("mm4fin", k))

            block.tensor(t_4)

            def v_4(v, k=k):
                for oc in range(8):
                    bset = pk[0:4] if oc % 2 == 0 else pk[4:8]
                    for tt in range(4):
                        idx = oc * 4 + tt
                        em.wait(v, "MM", ("mm4", k, oc, tt))
                        em.wait(v, "ST", ("h4", k, oc, min(tt + 1, 3)))
                        if idx >= 2:
                            em.wait(v, "ST", ("yst", k, (idx - 2) // 4,
                                              (idx - 2) % 4))
                        elif k == 0:
                            em.wait(v, "ST", ("rst", NSLAB - 1, 7, 3))
                        else:
                            em.wait(v, "ST", ("yst", k - 1, 7, 2 + idx))
                        cp = idx % 2
                        v.scalar_tensor_tensor(
                            yCh[:, cp * 512:(cp + 1) * 512],
                            bset[tt][:, 0:512], 1.0,
                            hRes[:, cp * 512:(cp + 1) * 512],
                            ALU.mult, ALU.add,
                            accum_out=state[:, S_SUM + tt * 8 + oc:
                                            S_SUM + tt * 8 + oc + 1])
                        ins = v.scalar_tensor_tensor(
                            sqJ4[:], yCh[:, cp * 512:(cp + 1) * 512], 1.0,
                            yCh[:, cp * 512:(cp + 1) * 512],
                            ALU.mult, ALU.mult,
                            accum_out=state[:, S_SQ + tt * 8 + oc:
                                            S_SQ + tt * 8 + oc + 1])
                        em.inc(ins, "EVV", 1)
                        em.mark("EVV", ("ev4", k, oc, tt))
                if k >= 2:
                    em.wait(v, "EVA", ("app4", k - 2, 31))
                v.drain()
                MUB = S_MU2_P[k % 2]
                RSB = S_RS2_P[k % 2]
                for tt in range(4):
                    mu = state[:, MUB + tt:MUB + tt + 1]
                    va = state[:, S_T1:S_T1 + 1]
                    v.tensor_reduce(mu, state[:, S_SUM + tt * 8:S_SUM + tt * 8 + 8],
                                    AX.X, ALU.add)
                    v.tensor_reduce(va, state[:, S_SQ + tt * 8:S_SQ + tt * 8 + 8],
                                    AX.X, ALU.add)
                    v.drain()
                    v.tensor_single_scalar(mu, mu, 1.0 / D, ALU.mult)
                    v.tensor_single_scalar(va, va, 1.0 / D, ALU.mult)
                    v.drain()
                    v.tensor_tensor(state[:, S_T2:S_T2 + 1], mu, mu, ALU.mult)
                    v.drain()
                    v.tensor_tensor(va, va, state[:, S_T2:S_T2 + 1], ALU.subtract)
                    v.drain()
                    ins = v.tensor_single_scalar(
                        state[:, RSB + tt:RSB + tt + 1], va, LN_EPS, ALU.add)
                    em.inc(ins, "EVV", 1)
                    em.mark("EVV", ("var4", k, tt))

            block.vector(v_4)

            def s_ln2(s, k=k):
                RSB = S_RS2_P[k % 2]
                for tt in range(4):
                    em.wait(s, "EVV", ("var4", k, tt))
                    ins = s.activation(state[:, RSB + tt:RSB + tt + 1],
                                       state[:, RSB + tt:RSB + tt + 1], AF.Sqrt)
                    em.inc(ins, "EVA", 1)
                    em.mark("EVA", ("sqrt4", k, tt))

            block.scalar(s_ln2)

            def v_fin2(v, k=k):
                MUB = S_MU2_P[k % 2]
                RSB = S_RS2_P[k % 2]
                BIB = S_BI2_P[k % 2]
                for tt in range(4):
                    em.wait(v, "EVA", ("sqrt4", k, tt))
                    rs = state[:, RSB + tt:RSB + tt + 1]
                    v.reciprocal(rs, rs)
                    v.drain()
                    ins = v.scalar_tensor_tensor(
                        state[:, BIB + tt:BIB + tt + 1],
                        state[:, MUB + tt:MUB + tt + 1],
                        -1.0, rs, ALU.mult, ALU.mult)
                    em.inc(ins, "EVV", 1)
                    em.mark("EVV", ("fin4", k, tt))

            block.vector(v_fin2)

            def s_apply2(s, k=k):
                RSB = S_RS2_P[k % 2]
                BIB = S_BI2_P[k % 2]
                for tt in range(4):
                    em.wait(s, "EVV", ("fin4", k, tt))
                    for ds in range(8):
                        idx = tt * 8 + ds
                        em.wait(s, "ST", ("yl", k, idx))
                        if idx >= 2:
                            em.wait(s, "ST", ("yo", k, idx - 2))
                        elif k == 0:
                            em.wait(s, "HS", ("hs", NSLAB - 1, 31))
                            em.wait(s, "STG", ("hc", NSLAB - 1, 31))
                        else:
                            em.wait(s, "ST", ("yo", k - 1, 30 + idx))
                        ins = s.activation(
                            yoCh[:, (idx % 2) * 512:(idx % 2 + 1) * 512],
                            ylCh[:, (idx % 2) * 512:(idx % 2 + 1) * 512],
                            AF.Identity,
                            bias=state[:, BIB + tt:BIB + tt + 1],
                            scale=state[:, RSB + tt:RSB + tt + 1])
                        em.inc(ins, "EVA", 1)
                        em.mark("EVA", ("app4", k, idx))

            block.scalar(s_apply2)

        def sy_fin(sy):
            em.waitv(sy, "RL", em.n["RL"])
            em.waitv(sy, "HS", em.n["HS"])
            em.waitv(sy, "SEL", em.n["SEL"])
            em.waitv(sy, "ST", em.n["ST"])
            em.waitv(sy, "STG", em.n["STG"])
            em.waitv(sy, "LDT", em.n["LDT"])
            em.waitv(sy, "LDW", em.n["LDW"])

        block.sync(sy_fin)

    return nc


def build_program(n_tok=2048, taps=False, stages=4):
    rec = {}
    _emit(n_tok, taps, rec, final=False, stages=stages)
    return _emit(n_tok, taps, rec, final=True, stages=stages)


def _pack_gw(gwT):
    out = np.zeros((128, NJ * 128), dtype=np.float32)
    g = gwT.reshape(NJ, 128, 8)
    for j in range(NJ):
        out[:, j * 128:j * 128 + 8] = g[j]
    return out


def kernel(L0_attn_x, L0_attn_w_gate, L0_attn_w_up, L0_attn_w_down,
           L0_ffn_gate_w, L0_ffn_base_w, L0_ffn_base_b,
           L0_ffn_lora_A, L0_ffn_lora_B):
    n_cores = 8
    B, S, _ = L0_attn_x.shape
    ntok = B * S
    per = ntok // n_cores
    x2 = np.ascontiguousarray(np.asarray(L0_attn_x, dtype=np.float32).reshape(ntok, D))

    def T(a):
        return np.ascontiguousarray(np.asarray(a, dtype=np.float32).T)

    consts = {
        "wgT": T(L0_attn_w_gate), "wuT": T(L0_attn_w_up), "wdT": T(L0_attn_w_down),
        "wbT": T(L0_ffn_base_w),
        "gwP": _pack_gw(T(L0_ffn_gate_w)),
        "laT": T(np.asarray(L0_ffn_lora_A, dtype=np.float32).reshape(128, D)),
        "lbT": np.ascontiguousarray(
            np.transpose(np.asarray(L0_ffn_lora_B, dtype=np.float32),
                         (0, 2, 1)).reshape(128, D)),
        "bias": np.asarray(L0_ffn_base_b, dtype=np.float32).reshape(1, D),
        "ident": np.eye(128, dtype=np.float32),
        "expm": np.repeat(np.eye(8, dtype=np.float32), 16, axis=1).reshape(8, 128),
        "ones": np.concatenate([np.ones((1, 128), np.float32),
                                np.zeros((127, 128), np.float32)]),
    }
    nc = build_program(n_tok=per, taps=False)
    in_maps = []
    for c in range(n_cores):
        xs = x2[c * per:(c + 1) * per]
        m = {"x": xs, "xT": np.ascontiguousarray(xs.T)}
        m.update(consts)
        in_maps.append(m)
    res = run_bass_kernel_spmd(nc, in_maps, core_ids=list(range(n_cores)))
    y = np.concatenate([np.asarray(r["y"], dtype=np.float32) for r in res.results],
                       axis=0)
    return y.reshape(B, S, D)


# version 05 (final)
# speedup vs baseline: 1.3750x; 1.2631x over previous
import os
import numpy as np

import concourse.bass as bass
import concourse.mybir as mybir
from concourse.bass_utils import run_bass_kernel_spmd

F32 = mybir.dt.float32
F16 = mybir.dt.float16
AF = mybir.ActivationFunctionType
ALU = mybir.AluOpType
AX = mybir.AxisListType

D = 4096
I = 11008
NJ = D // 128
NI = I // 128
IC = 256
NCH = I // IC
NOCT = (NI + 7) // 8
KSEL = 5504.0
SUBW = 2048
KSUB = KSEL * SUBW / I
CSCALE = float(I) / SUBW
NPS = 7
NPF = 5
NPROBE = NPS + NPF
LN_EPS = 1e-5
SCALING = 2.0
ACENTER = 0.28125
SEMS = ["LDW", "LDT", "SEL", "MM", "EVV", "EVA", "ST", "STG", "RL", "HS"]


class Em:
    def __init__(self, nc, rec, final):
        self.nc, self.rec, self.final = nc, rec, final
        self.h, self.n = {}, {}
        for nm in SEMS:
            self.h[nm] = nc.semaphore(nm).__enter__()
            self.n[nm] = 0

    def inc(self, ins, sem, by=1):
        self.n[sem] += by
        ins.then_inc(self.h[sem], by)

    def mark(self, sem, key):
        if self.final:
            assert self.rec[key] == self.n[sem], (key, self.rec[key], self.n[sem])
        else:
            self.rec[key] = self.n[sem]

    def wait(self, eng, sem, key):
        if self.final:
            eng.wait_ge(self.h[sem], self.rec[key])

    def waitv(self, eng, sem, v):
        if self.final:
            eng.wait_ge(self.h[sem], v)


def _emit(n_tok, taps, rec, final, stages=4):
    NT = n_tok // 128
    NSLAB = n_tok // 512
    THALF = n_tok // 2
    NTH = THALF // 128
    assert NTH % 2 == 0 and NSLAB >= 1

    nc = bass.Bass("TRN2", target_bir_lowering=False)

    x_in = nc.dram_tensor("x", [n_tok, D], F32, kind="ExternalInput").ap()
    xT_in = nc.dram_tensor("xT", [D, n_tok], F32, kind="ExternalInput").ap()
    wgT = nc.dram_tensor("wgT", [D, I], F32, kind="ExternalInput").ap()
    wuT = nc.dram_tensor("wuT", [D, I], F32, kind="ExternalInput").ap()
    wdT = nc.dram_tensor("wdT", [I, D], F32, kind="ExternalInput").ap()
    wbT = nc.dram_tensor("wbT", [D, D], F32, kind="ExternalInput").ap()
    gwP = nc.dram_tensor("gwP", [128, NJ * 128], F32, kind="ExternalInput").ap()
    laT = nc.dram_tensor("laT", [D, 128], F32, kind="ExternalInput").ap()
    lbT = nc.dram_tensor("lbT", [128, D], F32, kind="ExternalInput").ap()
    bias_in = nc.dram_tensor("bias", [1, D], F32, kind="ExternalInput").ap()
    ident_in = nc.dram_tensor("ident", [128, 128], F32, kind="ExternalInput").ap()
    expm_in = nc.dram_tensor("expm", [8, 128], F32, kind="ExternalInput").ap()
    ones_in = nc.dram_tensor("ones", [128, 128], F32, kind="ExternalInput").ap()

    y_out = nc.dram_tensor("y", [n_tok, D], F32, kind="ExternalOutput").ap()

    skind = "ExternalOutput" if taps else "Internal"
    a_scr = nc.dram_tensor("a_scr", [n_tok, I], F16, kind=skind).ap()
    p0_scr = nc.dram_tensor("p0_scr", [n_tok, I], F16, kind=skind).ap()
    p_scr = nc.dram_tensor("p_scr", [n_tok, I], F16, kind=skind).ap()
    r_scr = nc.dram_tensor("r_scr", [n_tok, D], F32, kind=skind).ap()
    h_scr = nc.dram_tensor("h_scr", [n_tok, D], F32, kind=skind).ap()
    h_f16 = nc.dram_tensor("h_f16", [n_tok, D], F16, kind=skind).ap()
    y_scr = nc.dram_tensor("y_scr", [n_tok, D], F32, kind=skind).ap()
    wd2 = nc.dram_tensor("wd2", [8, NOCT, 128, 8, 512], F16, kind="Internal").ap()
    wb2 = nc.dram_tensor("wb2", [8, 128, NJ, 512], F16, kind="Internal").ap()
    bias_f16 = nc.dram_tensor("bias_f16", [1, D], F16, kind="Internal").ap()
    if taps:
        tsel = nc.dram_tensor("tsel", [n_tok, 96], F32, kind="ExternalOutput").ap()
        dbg4 = nc.dram_tensor("dbg4", [n_tok // 512, 128, 1152], F32,
                              kind="ExternalOutput").ap()
        tsel2 = nc.dram_tensor("tsel2", [n_tok, 96], F32,
                               kind="ExternalOutput").ap()

    SB_BASE = 20480

    def A(name, shape, dtype, offset):
        return nc.alloc_sbuf_tensor_at(name, shape, dtype, offset=offset + SB_BASE)

    ident32 = A("ident32", [128, 128], F32, offset=0).ap()
    state = A("state", [128, 256], F32, offset=512).ap()
    sml = A("sml", [128, 256], F32, offset=1536).ap()
    P0 = 2560
    xTb = A("xTb", [128, NJ * THALF], F16, offset=P0).ap()
    o = P0 + NJ * THALF * 2
    guB = A("guB", [128, 2 * NJ * 512], F16, offset=o).ap()
    o += 2 * NJ * 512 * 2
    sRing = A("sRing", [128, 2 * IC], F32, offset=o).ap()
    o += 2 * IC * 4
    aStrip = A("aStrip", [128, 2 * NTH * IC], F16, offset=o).ap()
    o += 2 * NTH * IC * 2
    p0Strip = A("p0Strip", [128, 2 * NTH * IC], F16, offset=o).ap()
    o += 2 * NTH * IC * 2
    xsc = A("xsc", [128, 2 * IC], F32, offset=o).ap()
    st1_end = o + 2 * IC * 4
    pTs = A("pTs", [128, NI * 512], F16, offset=P0).ap()
    o2 = P0 + NI * 512 * 2
    wdR = A("wdR", [128, 2 * 8 * 512], F16, offset=o2).ap()
    o2 += 2 * 8 * 512 * 2
    xCh = A("xCh", [128, 2 * 512], F32, offset=o2).ap(); o2 += 2 * 512 * 4
    rCh = A("rCh", [128, 2 * 512], F32, offset=o2).ap(); o2 += 2 * 512 * 4
    sqJ = A("sqJ", [128, 512], F16, offset=o2).ap(); o2 += 512 * 2
    rlCh = A("rlCh", [128, 2 * 512], F32, offset=o2).ap(); o2 += 2 * 512 * 4
    hCh = A("hCh", [128, 2 * 512], F32, offset=o2).ap(); o2 += 2 * 512 * 4
    wbB = A("wbB", [128, 2 * NJ * 512], F16, offset=P0).ap()
    o4 = P0 + 2 * NJ * 512 * 2
    hTb = A("hTb", [128, NJ * 512], F16, offset=o4).ap()
    o4 += NJ * 512 * 2
    assert o4 <= P0 + NI * 512 * 2 + 2 * 8 * 512 * 2, o4
    o4 = o2
    wdR3 = A("wdR3", [128, 8 * 512], F16, offset=o4).ap()
    gwB = A("gwB", [128, NJ * 128], F16, offset=o4).ap(); o4 += NJ * 128 * 2
    laB = A("laB", [128, NJ * 128], F16, offset=o4).ap(); o4 += NJ * 128 * 2
    lbB = A("lbB", [128, D], F16, offset=o4).ap(); o4 += D * 2
    biasOC = A("biasOC", [128, 2 * 512], F16, offset=o4).ap(); o4 += 2 * 512 * 2
    expB = A("expB", [8, 128], F16, offset=o4).ap(); o4 += 128 * 2
    onesB = A("onesB", [128, 128], F16, offset=o4).ap(); o4 += 128 * 2
    glT = A("glT", [8, 512], F32, offset=o4).ap(); o4 += 512 * 4
    glog = A("glog", [128, 32], F32, offset=o4).ap(); o4 += 32 * 4
    wful = A("wful", [128, 32], F32, offset=o4).ap(); o4 += 32 * 4
    wfT = A("wfT", [8, 512], F16, offset=o4).ap(); o4 += 512 * 2
    lowS = A("lowS", [128, 512], F32, offset=o4).ap(); o4 += 512 * 4
    lsT = A("lsT", [128, 512], F16, offset=o4).ap(); o4 += 512 * 2
    hRes, yCh, ylCh, yoCh, sqJ4 = xCh, rCh, rlCh, hCh, sqJ
    SELOFF = max(st1_end, o4)
    aT = A("aT", [128, I], F16, offset=SELOFF).ap()
    wrk = A("wrk", [128, I], F16, offset=SELOFF + I * 2).ap()
    SEL_END = SELOFF + 2 * I * 2
    assert SEL_END <= 208767, SEL_END

    S_LOHI, S_SLOHI, S_TH, S_CNT, S_MV, S_T1, S_T2 = 0, 2, 4, 5, 6, 10, 12
    S_E, S_DEN, S_N2 = 14, 15, 16
    S_SUM, S_SQ = 18, 50
    S_MU, S_RS, S_BI = 82, 86, 90
    S_M1, S_M2, S_T3 = 94, 98, 102
    S_BTH, S_BERR, S_BT1, S_BT2 = 104, 105, 106, 107
    S_MU_O, S_RS_O, S_BI_O = 108, 112, 116
    S_MU_P = (82, 108)
    S_RS_P = (86, 112)
    S_BI_P = (90, 116)
    S_MU2_P = (120, 132)
    S_RS2_P = (124, 136)
    S_BI2_P = (128, 140)

    pk = [nc.alloc_psum_tensor(f"bank{i}", [128, 512], F32).ap() for i in range(8)]

    em = Em(nc, rec, final)

    with nc.Block() as block:
        def g_prelude(g):
            ins = g.dma_start(out=ident32[:], in_=ident_in)
            em.inc(ins, "LDW", 16)
            em.mark("LDW", "ident")

        block.gpsimd(g_prelude)

        def sel_ld_a(q, t):
            half_needed = 0 if (t + 1) * 128 <= THALF else 1
            em.wait(q, "ST", ("s1st", half_needed, NCH - 1))
            if t >= 1:
                em.wait(q, "EVV", ("selmask", t - 1))
            ins = q.dma_start(out=aT[:], in_=a_scr[t * 128:(t + 1) * 128, :])
            em.inc(ins, "SEL", 16)
            em.mark("SEL", ("sella", t))

        def sel_ld_p0(q, t):
            em.wait(q, "EVV", ("selprobed", t))
            if t >= 1:
                em.wait(q, "SEL", ("selstore", t - 1))
            ins = q.dma_start(out=wrk[:], in_=p0_scr[t * 128:(t + 1) * 128, :])
            em.inc(ins, "SEL", 16)
            em.mark("SEL", ("selp0", t))

        def sel_store(q, t):
            em.wait(q, "EVV", ("selmask", t))
            ins = q.dma_start(out=p_scr[t * 128:(t + 1) * 128, :], in_=wrk[:])
            em.inc(ins, "SEL", 16)
            em.mark("SEL", ("selstore", t))
            if taps:
                ins = q.dma_start(out=tsel[t * 128:(t + 1) * 128, :],
                                  in_=state[:, 160:256])
                em.inc(ins, "SEL", 16)
                ins = q.dma_start(out=tsel2[t * 128:(t + 1) * 128, :],
                                  in_=sml[:, 0:96])
                em.inc(ins, "SEL", 16)

        def sel_pieces_for_tile(t, sink):
            lohi = state[:, S_LOHI:S_LOHI + 2]
            slohi = state[:, S_SLOHI:S_SLOHI + 2]
            th = state[:, S_TH:S_TH + 1]
            cnt = state[:, S_CNT:S_CNT + 1]
            mvq = state[:, S_MV:S_MV + 4]
            t1 = state[:, S_T1:S_T1 + 2]
            t2 = state[:, S_T2:S_T2 + 2]
            e = state[:, S_E:S_E + 1]
            den = state[:, S_DEN:S_DEN + 1]
            n2 = state[:, S_N2:S_N2 + 1]

            def p_init(v, t=t):
                v.memset(lohi[:, 0:1], -1.0)
                v.memset(lohi[:, 1:2], 32.0)
                v.memset(slohi[:, 0:1], float(SUBW))
                v.memset(slohi[:, 1:2], 0.0)
                v.memset(th, 0.2 - ACENTER)
                v.memset(state[:, S_BTH:S_BTH + 1], 0.2 - ACENTER)
                v.memset(state[:, S_BERR:S_BERR + 1], 1e30)
                v.drain()

            sink.append(("init", t, p_init))

            for it in range(NPROBE):
                W = I if it >= NPS else SUBW
                K = KSEL if it >= NPS else KSUB

                def p_probe(v, t=t, it=it, W=W, K=K):
                    if it == 0:
                        em.wait(v, "SEL", ("sella", t))
                        if t >= 1:
                            em.wait(v, "SEL", ("selstore", t - 1))
                    v.tensor_scalar(wrk[:, 0:W], aT[:, 0:W], th, 0.0,
                                    ALU.is_ge, ALU.add, accum_out=cnt)
                    v.drain()
                    if it >= NPS:
                        bth = state[:, S_BTH:S_BTH + 1]
                        berr = state[:, S_BERR:S_BERR + 1]
                        bt1 = state[:, S_BT1:S_BT1 + 1]
                        bt2 = state[:, S_BT2:S_BT2 + 1]
                        v.tensor_single_scalar(bt1, cnt, K, ALU.subtract)
                        v.drain()
                        v.tensor_tensor(bt1, bt1, bt1, ALU.mult)
                        v.drain()
                        v.tensor_tensor(bt2, bt1, berr, ALU.is_lt)
                        v.drain()
                        v.tensor_tensor(berr, berr, bt1, ALU.min)
                        v.tensor_tensor(bt1, th, bth, ALU.subtract)
                        v.drain()
                        v.tensor_tensor(bt1, bt1, bt2, ALU.mult)
                        v.drain()
                        v.tensor_tensor(bth, bth, bt1, ALU.add)
                        v.drain()
                    v.tensor_single_scalar(mvq[:, 0:1], cnt, K, ALU.is_ge)
                    v.drain()
                    v.tensor_scalar(mvq[:, 1:2], mvq[:, 0:1], -1.0, 1.0,
                                    ALU.mult, ALU.add)
                    v.drain()
                    v.tensor_copy(mvq[:, 2:3], mvq[:, 1:2])
                    v.tensor_copy(mvq[:, 3:4], mvq[:, 0:1])
                    v.drain()
                    v.tensor_scalar(t1, lohi, th, None, ALU.subtract)
                    v.drain()
                    v.tensor_tensor(t2, mvq[:, 0:2], t1, ALU.mult)
                    v.drain()
                    v.tensor_tensor(lohi, lohi, t2, ALU.subtract)
                    v.tensor_scalar(t1, slohi, cnt, None, ALU.subtract)
                    v.drain()
                    v.tensor_tensor(t2, mvq[:, 0:2], t1, ALU.mult)
                    v.drain()
                    v.tensor_tensor(slohi, slohi, t2, ALU.subtract)
                    v.drain()
                    v.tensor_single_scalar(t1, slohi, K, ALU.subtract)
                    v.drain()
                    v.tensor_tensor(t1, t1, mvq[:, 2:4], ALU.mult)
                    v.drain()
                    v.tensor_single_scalar(t1, t1, 0.5, ALU.mult)
                    v.drain()
                    v.tensor_tensor(slohi, slohi, t1, ALU.subtract)
                    v.drain()
                    if it == NPS - 1:
                        v.tensor_single_scalar(lohi[:, 0:1], th, -0.008, ALU.add)
                        v.tensor_single_scalar(lohi[:, 1:2], th, 0.008, ALU.add)
                        v.memset(slohi[:, 0:1], KSEL + 300.0)
                        v.memset(slohi[:, 1:2], KSEL - 300.0)
                        v.drain()
                    if it == 0:
                        v.memset(th, 0.45 - ACENTER)
                        v.drain()
                    elif it < NPROBE - 1:
                        v.tensor_tensor(e, lohi[:, 1:2], lohi[:, 0:1], ALU.subtract)
                        v.tensor_tensor(den, slohi[:, 0:1], slohi[:, 1:2],
                                        ALU.subtract)
                        v.drain()
                        v.reciprocal(den, den)
                        v.drain()
                        v.tensor_tensor(e, e, den, ALU.mult)
                        v.drain()
                        kk = KSEL if it >= NPS - 1 else KSUB
                        v.tensor_single_scalar(n2, slohi[:, 0:1], kk, ALU.subtract)
                        v.drain()
                        v.tensor_tensor(e, e, n2, ALU.mult)
                        v.drain()
                        v.tensor_tensor(th, lohi[:, 0:1], e, ALU.add)
                        v.drain()
                    if it == NPROBE - 1:
                        ins = v.tensor_copy(state[:, S_T3:S_T3 + 1],
                                            state[:, S_BTH:S_BTH + 1])
                        em.inc(ins, "EVV", 1)
                        em.mark("EVV", ("selprobed", t))

                sink.append(("probe", t, p_probe))

            def p_mask(v, t=t):
                em.wait(v, "SEL", ("selp0", t))
                if taps:
                    c6 = 160 + 6 * (t % 16)
                    v.tensor_copy(state[:, c6:c6 + 4], state[:, 0:4])
                    v.tensor_copy(state[:, c6 + 4:c6 + 5],
                                  state[:, S_BTH:S_BTH + 1])
                    v.tensor_copy(state[:, c6 + 5:c6 + 6],
                                  state[:, S_BERR:S_BERR + 1])
                    v.tensor_copy(sml[:, 0:48], aT[:, 64:112])
                    v.tensor_copy(sml[:, 48:72], wrk[:, 64:88])
                    v.drain()
                v.scalar_tensor_tensor(wrk[:], aT[:],
                                       state[:, S_BTH:S_BTH + 1], wrk[:],
                                       ALU.is_ge, ALU.mult)
                if taps:
                    v.drain()
                    v.tensor_copy(sml[:, 72:96], wrk[:, 64:88])
                ins = v.tensor_copy(state[:, S_T3 + 1:S_T3 + 2],
                                    state[:, S_BTH:S_BTH + 1])
                em.inc(ins, "EVV", 1)
                em.mark("EVV", ("selmask", t))

            sink.append(("mask", t, p_mask))

        NBUF = 2

        prep_jobs = []
        for ds_ in range(8):
            for op_ in range(NOCT):
                i0_, i1_ = op_ * 8, min(op_ * 8 + 8, NI)

                def pj_wd(g, ds_=ds_, op_=op_, i0_=i0_, i1_=i1_):
                    ins = g.dma_start(
                        out=wd2[ds_, op_, :, 0:i1_ - i0_, :],
                        in_=wdT[i0_ * 128:i1_ * 128,
                                ds_ * 512:(ds_ + 1) * 512].rearrange(
                                    "(b p) d -> p b d", p=128))
                    em.inc(ins, "LDW", 16)

                prep_jobs.append(pj_wd)
        for oc_ in range(8):

            def pj_wb(g, oc_=oc_):
                ins = g.dma_start(
                    out=wb2[oc_],
                    in_=wbT[:, oc_ * 512:(oc_ + 1) * 512].rearrange(
                        "(j p) w -> p j w", p=128))
                em.inc(ins, "LDW", 16)

            prep_jobs.append(pj_wb)

        def pj_bias(g):
            ins = g.dma_start(out=bias_f16, in_=bias_in)
            em.inc(ins, "LDW", 16)

        prep_jobs.append(pj_bias)
        assert len(prep_jobs) <= 3 * (NCH - NBUF), len(prep_jobs)

        def guv(b):
            return guB[:, b * NJ * 512:(b + 1) * NJ * 512].rearrange(
                "p (j w) -> p j w", j=NJ)

        def st1_load_chunk(g, half, c):
            gl = half * NCH + c
            if gl >= NBUF:
                ph, pc = divmod(gl - NBUF, NCH)
                em.wait(g, "MM", ("s1mm", ph, pc, NTH - 1))
            v = guv(gl % NBUF)
            ins = g.dma_start(
                out=v[:, :, 0:IC],
                in_=wgT[:, c * IC:(c + 1) * IC].rearrange("(j p) i -> p j i", p=128))
            em.inc(ins, "LDW", 16)
            ins = g.dma_start(
                out=v[:, :, IC:2 * IC],
                in_=wuT[:, c * IC:(c + 1) * IC].rearrange("(j p) i -> p j i", p=128))
            em.inc(ins, "LDW", 16)
            em.mark("LDW", ("s1ld", half, c))

        def emit_st1(half, sel_pieces, sync_sched):
            def g_loads(g):
                if half == 1:
                    em.wait(g, "MM", ("s1mm", 0, NCH - 1, NTH - 1))
                src = xT_in[:, half * THALF:(half + 1) * THALF].rearrange(
                    "(j p) t -> p j t", p=128)
                ins = g.dma_start(out=xTb[:].rearrange("p (j t) -> p j t", j=NJ),
                                  in_=src)
                em.inc(ins, "LDW", 16)
                em.mark("LDW", ("s1x", half))
                for c in range(NCH):
                    st1_load_chunk(g, half, c)
                    if half == 0 and c >= NBUF:
                        for _ in range(3):
                            if prep_jobs:
                                prep_jobs.pop(0)(g)
                if half == 0:
                    assert not prep_jobs, len(prep_jobs)
                    em.mark("LDW", "preconv")

            block.gpsimd(g_loads)

            def t_mms(t):
                em.wait(t, "LDW", ("s1x", half))
                for c in range(NCH):
                    v = guv((half * NCH + c) % NBUF)
                    em.wait(t, "LDW", ("s1ld", half, c))
                    for tt in range(NTH):
                        bank = pk[tt % 2]
                        gl = (half * NCH + c) * NTH + tt
                        if gl >= 2:
                            pg = gl - 2
                            ph, pr = divmod(pg, NCH * NTH)
                            pc, ptt = divmod(pr, NTH)
                            em.wait(t, "EVV", ("s1ev", ph, pc, ptt))
                        for j in range(NJ):
                            ins = t.matmul(
                                bank[:, 0:512],
                                xTb[:, j * THALF + tt * 128:
                                    j * THALF + (tt + 1) * 128],
                                v[:, j, :], start=(j == 0), stop=(j == NJ - 1))
                        em.inc(ins, "MM", 1)
                        em.mark("MM", ("s1mm", half, c, tt))

            block.tensor(t_mms)

            def s_silu(s):
                for c in range(NCH):
                    par = (half * NCH + c) % 2
                    for tt in range(NTH):
                        bank = pk[tt % 2]
                        em.wait(s, "MM", ("s1mm", half, c, tt))
                        gl = (half * NCH + c) * NTH + tt
                        if gl >= 2:
                            pg = gl - 2
                            ph, pr = divmod(pg, NCH * NTH)
                            pc, ptt = divmod(pr, NTH)
                            em.wait(s, "EVV", ("s1ev", ph, pc, ptt))
                        sr = sRing[:, (tt % 2) * IC:(tt % 2 + 1) * IC]
                        ins = s.activation(sr, bank[:, 0:IC],
                                           AF.Identity if os.environ.get('ZTAP') else AF.Silu)
                        em.inc(ins, "EVA", 1)
                        em.mark("EVA", ("s1silu", half, c, tt))

            block.scalar(s_silu)

            def v_evac(v):
                for c in range(NCH):
                    par = (half * NCH + c) % 2
                    for tt in range(NTH):
                        bank = pk[tt % 2]
                        em.wait(v, "EVA", ("s1silu", half, c, tt))
                        gl = half * NCH + c
                        if gl >= 2:
                            ph, pc = divmod(gl - 2, NCH)
                            em.wait(v, "ST", ("s1st", ph, pc))
                        sr = sRing[:, (tt % 2) * IC:(tt % 2 + 1) * IC]
                        v.tensor_tensor(
                            p0Strip[:, (par * NTH + tt) * IC:
                                    (par * NTH + tt + 1) * IC],
                            sr, bank[:, IC:2 * IC], ALU.mult)
                        v.scalar_tensor_tensor(xsc[:, (tt % 2) * IC:
                                                   (tt % 2 + 1) * IC],
                                               sr, -1.0, sr, ALU.mult, ALU.max)
                        v.drain()
                        ins = v.tensor_single_scalar(
                            aStrip[:, (par * NTH + tt) * IC:
                                   (par * NTH + tt + 1) * IC],
                            xsc[:, (tt % 2) * IC:(tt % 2 + 1) * IC],
                            ACENTER, ALU.subtract)
                        em.inc(ins, "EVV", 1)
                        em.mark("EVV", ("s1ev", half, c, tt))
                        if half == 1 and sel_pieces and tt % 2 == 0:
                            kind, pt, fn = sel_pieces.pop(0)
                            fn(v)
                            cc = max(0, c - 1)
                            if kind == "probe":
                                pieces_left = [p for p in sel_pieces
                                               if p[1] == pt and p[0] == "probe"]
                                if not pieces_left:
                                    sync_sched.setdefault(cc, []).append(
                                        lambda q, pt=pt: sel_ld_p0(q, pt))
                                    p0_done.add(pt)
                            if kind == "mask":
                                sync_sched.setdefault(cc, []).append(
                                    lambda q, pt=pt: sel_store(q, pt))
                                st_done.add(pt)
                                if pt + 1 in early_tiles:
                                    sync_sched.setdefault(cc, []).append(
                                        lambda q, pt=pt: sel_ld_a(q, pt + 1))
                                    lda_done.add(pt + 1)

            block.vector(v_evac)

            def sy_store(sy):
                for c in range(NCH):
                    par = (half * NCH + c) % 2
                    em.wait(sy, "EVV", ("s1ev", half, c, NTH - 1))
                    r0, r1 = half * THALF, (half + 1) * THALF
                    ins = sy.dma_start(
                        out=a_scr[r0:r1, c * IC:(c + 1) * IC].rearrange(
                            "(tt p) i -> p tt i", p=128),
                        in_=aStrip[:, par * NTH * IC:(par + 1) * NTH * IC].rearrange(
                            "p (tt i) -> p tt i", tt=NTH))
                    em.inc(ins, "ST", 16)
                    ins = sy.dma_start(
                        out=p0_scr[r0:r1, c * IC:(c + 1) * IC].rearrange(
                            "(tt p) i -> p tt i", p=128),
                        in_=p0Strip[:, par * NTH * IC:(par + 1) * NTH * IC].rearrange(
                            "p (tt i) -> p tt i", tt=NTH))
                    em.inc(ins, "ST", 16)
                    em.mark("ST", ("s1st", half, c))
                    for fn in sync_sched.get(c, []):
                        fn(sy)

            block.sync(sy_store)

        early_tiles = [t for t in range(min(NT, 8)) if (t + 1) * 128 <= THALF]
        if stages < 2 or os.environ.get("SELMODE", "") == "late":
            early_tiles = []
        late_tiles = [t for t in range(min(NT, 8)) if t not in early_tiles]
        if stages < 2 or os.environ.get("SELMODE", "") == "early":
            late_tiles = []
        early_pieces = []
        for t in early_tiles:
            sel_pieces_for_tile(t, early_pieces)
        sched = {}
        sched0 = {}
        lda_done, p0_done, st_done = set(), set(), set()
        if early_tiles:
            t0 = early_tiles[0]
            sched0.setdefault(NCH - 1, []).append(
                lambda q, t0=t0: sel_ld_a(q, t0))
            lda_done.add(t0)

        emit_st1(0, [], sched0)
        emit_st1(1, early_pieces, sched)
        if early_pieces:
            rest = list(early_pieces)
            early_pieces.clear()

            def v_left(v, ps=rest):
                for _, _, p in ps:
                    p(v)

            block.vector(v_left)
        for t in early_tiles:
            if t not in lda_done:
                block.sync(lambda q, t=t: sel_ld_a(q, t))
                lda_done.add(t)
            if t not in p0_done:
                block.sync(lambda q, t=t: sel_ld_p0(q, t))
                p0_done.add(t)
            if t not in st_done:
                block.sync(lambda q, t=t: sel_store(q, t))
                st_done.add(t)
        for t in late_tiles:
            block.sync(lambda q, t=t: sel_ld_a(q, t))
            lp = []
            sel_pieces_for_tile(t, lp)

            def v_lt(v, lp=lp):
                for _, _, p in lp:
                    p(v)

            block.vector(v_lt)
            block.sync(lambda q, t=t: sel_ld_p0(q, t))
            block.sync(lambda q, t=t: sel_store(q, t))

        WDSEM = ("ST", "ST", "ST", "LDW", "ST", "LDW", "ST", "LDW")

        def wdslot(idx, nb):
            s = idx % 3
            if s < 2:
                return wdR[:, s * 8 * 512:s * 8 * 512 + nb * 512]
            return wdR3[:, 0:nb * 512]

        def wd_load(q, sem, k, ds, o_):
            i0, i1 = o_ * 8, min(o_ * 8 + 8, NI)
            idx = ds * NOCT + o_
            if idx >= 3:
                em.wait(q, "MM", ("mmoct", k, (idx - 3) // NOCT,
                                  (idx - 3) % NOCT))
            elif k >= 1:
                em.wait(q, "MM", ("mmoct", k - 1, 7, NOCT - 1))
            elif idx == 0:
                em.wait(q, "LDW", "preconv")
            dst = wdslot(idx, i1 - i0).rearrange("p (b w) -> p b w", b=i1 - i0)
            ins = q.dma_start(out=dst, in_=wd2[ds, o_, :, 0:i1 - i0, :])
            em.inc(ins, sem, 16)
            em.mark(sem, ("wd", k, ds, o_))

        def sc_tp_body(sc, kk):
            rr = kk * 512
            for t in range(4):
                em.wait(sc, "SEL", ("selstore", 4 * kk + t))
            for i in range(NI):
                if kk >= 1:
                    em.wait(sc, "MM", ("mmoct", kk - 1, 7, min(i // 8, NOCT - 1)))
                ins = sc.dma_start_transpose(
                    pTs[:, i * 512:(i + 1) * 512],
                    p_scr[rr:rr + 512, i * 128:(i + 1) * 128])
                em.inc(ins, "LDT", 16)
                if i % 8 == 7:
                    em.mark("LDT", ("tpo", kk, i // 8))
            ins = sc.dma_start_transpose(
                pTs[:, (NI - 1) * 512:NI * 512],
                p_scr[rr:rr + 512, (NI - 1) * 128:NI * 128])
            em.inc(ins, "LDT", 16)
            em.mark("LDT", ("tpo", kk, (NI - 1) // 8))
            em.mark("LDT", ("tp", kk))

        def ln_io_g(g, kk):
            rr = kk * 512

            def hstore(pidx):
                ptt, pds = pidx // 8, pidx % 8
                em.wait(g, "EVA", ("app", kk, pidx))
                ins = g.dma_start(
                    out=h_scr[rr + ptt * 128:rr + ptt * 128 + 128,
                              pds * 512:(pds + 1) * 512],
                    in_=hCh[:, (pidx % 2) * 512:(pidx % 2 + 1) * 512])
                em.inc(ins, "HS", 16)
                em.mark("HS", ("hs", kk, pidx))
                ins = g.dma_start(
                    out=h_f16[rr + ptt * 128:rr + ptt * 128 + 128,
                              pds * 512:(pds + 1) * 512],
                    in_=hCh[:, (pidx % 2) * 512:(pidx % 2 + 1) * 512])
                em.inc(ins, "STG", 16)
                em.mark("STG", ("hc", kk, pidx))

            for tt in range(4):
                for ds in range(8):
                    idx = tt * 8 + ds
                    em.wait(g, "ST", ("rst", kk, ds, tt))
                    if idx >= 2:
                        em.wait(g, "EVA", ("app", kk, idx - 2))
                    elif kk >= 1:
                        em.wait(g, "EVA", ("app", kk - 1, 30 + idx))
                    ins = g.dma_start(
                        out=rlCh[:, (idx % 2) * 512:(idx % 2 + 1) * 512],
                        in_=r_scr[rr + tt * 128:rr + tt * 128 + 128,
                                  ds * 512:(ds + 1) * 512])
                    em.inc(ins, "RL", 16)
                    em.mark("RL", ("rl", kk, idx))
                    if idx >= 1:
                        hstore(idx - 1)
            hstore(31)

        for k in range(NSLAB if stages >= 3 else 0):
            r0 = k * 512

            pend = []
            pend_sched = {}
            sel_cycle_tiles = []
            if k + 2 < NSLAB:
                sel_cycle_tiles = list(range(4 * (k + 2), 4 * (k + 3)))
                for t in sel_cycle_tiles:
                    sel_pieces_for_tile(t, pend)
                t0 = sel_cycle_tiles[0]
                pend_sched.setdefault(-1, []).append(
                    lambda q, t0=t0: sel_ld_a(q, t0))

            if k == 0:
                block.scalar(lambda sc: sc_tp_body(sc, 0))

            S2LVL = int(os.environ.get("S2LVL", "4"))

            def sy_main(sy, k=k):
                if S2LVL < 2:
                    return
                for fn in pend_sched.get(-1, []):
                    fn(sy)
                for ds in range(8):
                    if ds in (0, 1, 2, 4, 6):
                        for o_ in range(NOCT):
                            wd_load(sy, "ST", k, ds, o_)
                    if ds >= 1:
                        for tt in range(4):
                            em.wait(sy, "EVV", ("ev2", k, ds - 1, tt))
                            pidx = (ds - 1) * 4 + tt
                            ins = sy.dma_start(
                                out=r_scr[r0 + tt * 128:r0 + tt * 128 + 128,
                                          (ds - 1) * 512:ds * 512],
                                in_=rCh[:, (pidx % 2) * 512:(pidx % 2 + 1) * 512])
                            em.inc(ins, "ST", 16)
                            em.mark("ST", ("rst", k, ds - 1, tt))
                    for fn in (pend_sched.get(ds - 1, []) if ds >= 1 else []):
                        fn(sy)
                    for tt in range(4):
                        idx = ds * 4 + tt
                        if idx >= 2:
                            em.wait(sy, "EVV", ("ev2", k, (idx - 2) // 4,
                                                (idx - 2) % 4))
                        elif k >= 1:
                            em.wait(sy, "EVV", ("ev2", k - 1, 7, 2 + idx))
                        ins = sy.dma_start(
                            out=xCh[:, (idx % 2) * 512:(idx % 2 + 1) * 512],
                            in_=x_in[r0 + tt * 128:r0 + tt * 128 + 128,
                                     ds * 512:(ds + 1) * 512])
                        em.inc(ins, "ST", 16)
                        em.mark("ST", ("x2", k, ds, tt))
                for tt in range(4):
                    em.wait(sy, "EVV", ("ev2", k, 7, tt))
                    pidx = 7 * 4 + tt
                    ins = sy.dma_start(
                        out=r_scr[r0 + tt * 128:r0 + tt * 128 + 128, 7 * 512:8 * 512],
                        in_=rCh[:, (pidx % 2) * 512:(pidx % 2 + 1) * 512])
                    em.inc(ins, "ST", 16)
                    em.mark("ST", ("rst", k, 7, tt))
                for b in (7, 99):
                    for fn in pend_sched.get(b, []):
                        fn(sy)
                if os.environ.get("S2LN", "1") == "0":
                    return

            def t_2b(t, k=k):
                if S2LVL < 3:
                    return
                if k >= 1:
                    em.wait(t, "EVV", ("ev2", k - 1, 7, 3))
                for ds in range(8):
                    bset = pk[0:4] if ds % 2 == 0 else pk[4:8]
                    if ds >= 2:
                        em.wait(t, "EVV", ("ev2", k, ds - 2, 3))
                    for o_ in range(NOCT):
                        i0, i1 = o_ * 8, min(o_ * 8 + 8, NI)
                        idx = ds * NOCT + o_
                        if ds == 0 and o_ == 0:
                            em.wait(t, "LDT", ("tp", k))
                        em.wait(t, WDSEM[ds], ("wd", k, ds, min(o_ + 1, NOCT - 1)))
                        for i in range(i0, i1):
                            slot = wdslot(idx, 8)
                            wv = slot[:, (i - i0) * 512:(i - i0 + 1) * 512]
                            for tt in range(4):
                                ins = t.matmul(
                                    bset[tt][:, 0:512],
                                    pTs[:, i * 512 + tt * 128:
                                        i * 512 + (tt + 1) * 128],
                                    wv, start=(i == 0), stop=(i == NI - 1))
                        em.inc(ins, "MM", 1)
                        em.mark("MM", ("mmoct", k, ds, o_))

            block.tensor(t_2b)

            def v_2b(v, k=k, pend=pend):
                if S2LVL < 4:
                    while pend:
                        pend.pop(0)[2](v)
                    return
                for ds in range(8):
                    bset = pk[0:4] if ds % 2 == 0 else pk[4:8]
                    for tt in range(4):
                        idx = ds * 4 + tt
                        em.wait(v, "MM", ("mmoct", k, ds, NOCT - 1))
                        em.wait(v, "ST", ("x2", k, ds, min(tt + 1, 3)))
                        if idx >= 2:
                            em.wait(v, "ST", ("rst", k, (idx - 2) // 4,
                                              (idx - 2) % 4))
                        elif k >= 1:
                            em.wait(v, "ST", ("rst", k - 1, 7, 2 + idx))
                        cp = idx % 2
                        acc1 = (state[:, S_SUM + tt * 8 + ds:
                                      S_SUM + tt * 8 + ds + 1]
                                if os.environ.get("S2ACC", "1") == "1" else None)
                        ins = v.scalar_tensor_tensor(
                            rCh[:, cp * 512:(cp + 1) * 512],
                            bset[tt][:, 0:512], 1.0,
                            xCh[:, cp * 512:(cp + 1) * 512],
                            ALU.mult, ALU.add,
                            accum_out=acc1)
                        if os.environ.get("S2SQ", "1") == "1":
                            acc2 = (state[:, S_SQ + tt * 8 + ds:
                                          S_SQ + tt * 8 + ds + 1]
                                    if os.environ.get("S2ACC", "1") == "1" else None)
                            ins = v.scalar_tensor_tensor(
                                sqJ[:], rCh[:, cp * 512:(cp + 1) * 512], 1.0,
                                rCh[:, cp * 512:(cp + 1) * 512],
                                ALU.mult, ALU.mult,
                                accum_out=acc2)
                        em.inc(ins, "EVV", 1)
                        em.mark("EVV", ("ev2", k, ds, tt))
                    for _ in range(7):
                        if pend:
                            kind, pt, fn = pend.pop(0)
                            fn(v)
                            if kind == "probe" and not any(
                                    p[1] == pt and p[0] == "probe" for p in pend):
                                pend_sched.setdefault(ds, []).append(
                                    lambda q, pt=pt: sel_ld_p0(q, pt))
                            if kind == "mask":
                                pend_sched.setdefault(ds, []).append(
                                    lambda q, pt=pt: sel_store(q, pt))
                                if pt + 1 in sel_cycle_tiles:
                                    pend_sched.setdefault(ds, []).append(
                                        lambda q, pt=pt: sel_ld_a(q, pt + 1))
                while pend:
                    kind, pt, fn = pend.pop(0)
                    fn(v)
                    if kind == "probe" and not any(
                            p[1] == pt and p[0] == "probe" for p in pend):
                        pend_sched.setdefault(99, []).append(
                            lambda q, pt=pt: sel_ld_p0(q, pt))
                    if kind == "mask":
                        pend_sched.setdefault(99, []).append(
                            lambda q, pt=pt: sel_store(q, pt))
                        if pt + 1 in sel_cycle_tiles:
                            pend_sched.setdefault(99, []).append(
                                lambda q, pt=pt: sel_ld_a(q, pt + 1))
                if os.environ.get("S2LN", "1") == "0":
                    return
                if k >= 2:
                    em.wait(v, "EVA", ("app", k - 2, 31))
                v.drain()
                MUB = S_MU if k % 2 == 0 else S_MU_O
                RSB = S_RS if k % 2 == 0 else S_RS_O
                for tt in range(4):
                    mu = state[:, MUB + tt:MUB + tt + 1]
                    va = state[:, S_T1:S_T1 + 1]
                    v.tensor_reduce(mu, state[:, S_SUM + tt * 8:S_SUM + tt * 8 + 8],
                                    AX.X, ALU.add)
                    v.tensor_reduce(va, state[:, S_SQ + tt * 8:S_SQ + tt * 8 + 8],
                                    AX.X, ALU.add)
                    v.drain()
                    v.tensor_single_scalar(mu, mu, 1.0 / D, ALU.mult)
                    v.tensor_single_scalar(va, va, 1.0 / D, ALU.mult)
                    v.drain()
                    v.tensor_tensor(state[:, S_T2:S_T2 + 1], mu, mu, ALU.mult)
                    v.drain()
                    v.tensor_tensor(va, va, state[:, S_T2:S_T2 + 1], ALU.subtract)
                    v.drain()
                    ins = v.tensor_single_scalar(
                        state[:, RSB + tt:RSB + tt + 1], va, LN_EPS, ALU.add)
                    em.inc(ins, "EVV", 1)
                    em.mark("EVV", ("var", k, tt))

            block.vector(v_2b)
            block.sync(sy_main)

            if k + 1 < NSLAB:
                block.scalar(lambda sc, kk=k + 1: sc_tp_body(sc, kk))

            def s_ln(s, k=k):
                RSB = S_RS if k % 2 == 0 else S_RS_O
                for tt in range(4):
                    em.wait(s, "EVV", ("var", k, tt))
                    ins = s.activation(state[:, RSB + tt:RSB + tt + 1],
                                       state[:, RSB + tt:RSB + tt + 1], AF.Sqrt)
                    em.inc(ins, "EVA", 1)
                    em.mark("EVA", ("sqrt", k, tt))

            if os.environ.get('S2LN', '1') == '1':
                block.scalar(s_ln)

            def v_fin(v, k=k):
                MUB = S_MU if k % 2 == 0 else S_MU_O
                RSB = S_RS if k % 2 == 0 else S_RS_O
                BIB = S_BI if k % 2 == 0 else S_BI_O
                for tt in range(4):
                    em.wait(v, "EVA", ("sqrt", k, tt))
                    rs = state[:, RSB + tt:RSB + tt + 1]
                    v.reciprocal(rs, rs)
                    v.drain()
                    ins = v.scalar_tensor_tensor(
                        state[:, BIB + tt:BIB + tt + 1],
                        state[:, MUB + tt:MUB + tt + 1],
                        -1.0, rs, ALU.mult, ALU.mult)
                    em.inc(ins, "EVV", 1)
                    em.mark("EVV", ("fin", k, tt))

            if os.environ.get('S2LN', '1') == '1':
                block.vector(v_fin)

            def s_apply(s, k=k):
                RSB = S_RS if k % 2 == 0 else S_RS_O
                BIB = S_BI if k % 2 == 0 else S_BI_O
                for tt in range(4):
                    em.wait(s, "EVV", ("fin", k, tt))
                    for ds in range(8):
                        idx = tt * 8 + ds
                        em.wait(s, "RL", ("rl", k, idx))
                        if idx >= 2:
                            em.wait(s, "HS", ("hs", k, idx - 2))
                            em.wait(s, "STG", ("hc", k, idx - 2))
                        elif k >= 1:
                            em.wait(s, "HS", ("hs", k - 1, 30 + idx))
                            em.wait(s, "STG", ("hc", k - 1, 30 + idx))
                        ins = s.activation(
                            hCh[:, (idx % 2) * 512:(idx % 2 + 1) * 512],
                            rlCh[:, (idx % 2) * 512:(idx % 2 + 1) * 512],
                            AF.Identity,
                            bias=state[:, BIB + tt:BIB + tt + 1],
                            scale=state[:, RSB + tt:RSB + tt + 1])
                        em.inc(ins, "EVA", 1)
                        em.mark("EVA", ("app", k, idx))

            if os.environ.get('S2LN', '1') == '1':
                block.scalar(s_apply)

            if os.environ.get('S2LN', '1') == '1' and k >= 1:
                block.gpsimd(lambda g, kk=k - 1: ln_io_g(g, kk))

            def g_wd(g, k=k):
                for ds in (3, 5, 7):
                    for o_ in range(NOCT):
                        wd_load(g, "LDW", k, ds, o_)

            block.gpsimd(g_wd)

        if stages >= 3 and NSLAB >= 1 and os.environ.get("S2LN", "1") == "1" \
                and int(os.environ.get("S2LVL", "4")) >= 2:
            block.gpsimd(lambda g: ln_io_g(g, NSLAB - 1))

        def g_smalls(g):
            em.wait(g, "MM", ("mmoct", NSLAB - 1, 7, NOCT - 1))
            em.inc(g.dma_start(out=expB[:], in_=expm_in), "LDW", 16)
            em.inc(g.dma_start(out=onesB[:], in_=ones_in), "LDW", 16)
            em.inc(g.dma_start(out=gwB[:], in_=gwP), "LDW", 16)
            em.inc(g.dma_start(out=laB[:].rearrange("p (j r) -> p j r", j=NJ),
                               in_=laT.rearrange("(j p) r -> p j r", p=128)),
                   "LDW", 16)
            em.inc(g.dma_start(out=lbB[:], in_=lbT), "LDW", 16)
            em.mark("LDW", "smalls")

        if stages >= 4:
            block.gpsimd(g_smalls)

        for k in range(NSLAB if stages >= 4 else 0):
            r0 = k * 512

            def sc_htp(sc, k=k):
                for idx in range(32):
                    em.wait(sc, "STG", ("hc", k, idx))
                if k >= 1:
                    em.wait(sc, "MM", ("mm4fin", k - 1))
                else:
                    em.wait(sc, "MM", ("mmoct", NSLAB - 1, 7, NOCT - 1))
                    em.wait(sc, "SEL", ("selstore", NT - 1))
                for j in range(NJ):
                    ins = sc.dma_start_transpose(
                        hTb[:, j * 512:(j + 1) * 512],
                        h_f16[r0:r0 + 512, j * 128:(j + 1) * 128])
                    em.inc(ins, "LDT", 16)
                em.mark("LDT", ("htp", k))

            block.scalar(sc_htp)

            def t_p1(t, k=k):
                em.wait(t, "LDW", "smalls")
                em.wait(t, "LDW", "ident")
                em.wait(t, "LDT", ("htp", k))
                if k >= 1:
                    em.wait(t, "EVV", ("ev4", k - 1, 7, 3))
                else:
                    em.wait(t, "EVV", ("ev2", NSLAB - 1, 7, 3))
                for j in range(NJ):
                    ins = t.matmul(pk[0][:, 0:512], gwB[:, j * 128:(j + 1) * 128],
                                   hTb[:, j * 512:(j + 1) * 512],
                                   start=(j == 0), stop=(j == NJ - 1))
                em.inc(ins, "MM", 1)
                em.mark("MM", ("mmgl", k))
                for j in range(NJ):
                    ins = t.matmul(pk[4][:, 0:512], laB[:, j * 128:(j + 1) * 128],
                                   hTb[:, j * 512:(j + 1) * 512],
                                   start=(j == 0), stop=(j == NJ - 1))
                em.inc(ins, "MM", 1)
                em.mark("MM", ("mmlow", k))

            block.tensor(t_p1)

            def s_p1(s, k=k):
                em.wait(s, "MM", ("mmgl", k))
                ins = s.copy(glT[:], pk[0][0:8, 0:512])
                em.inc(ins, "EVA", 1)
                em.mark("EVA", ("glev", k))

            block.scalar(s_p1)

            def t_p2(t, k=k):
                em.wait(t, "EVA", ("glev", k))
                for tt in range(4):
                    ins = t.transpose(pk[1][:, tt * 8:(tt + 1) * 8],
                                      glT[0:8, tt * 128:(tt + 1) * 128],
                                      ident32[0:8, 0:8])
                em.inc(ins, "MM", 1)
                em.mark("MM", ("mmglt", k))

            block.tensor(t_p2)

            def v_p1(v, k=k):
                if k == 0:
                    v.memset(biasOC[:], 0.0)
                    v.drain()
                em.wait(v, "MM", ("mmglt", k))
                v.tensor_copy(glog[:], pk[1][:, 0:32])
                v.drain()
                for tt in range(4):
                    g8 = glog[:, tt * 8:(tt + 1) * 8]
                    m1 = state[:, S_M1 + tt:S_M1 + tt + 1]
                    m2 = state[:, S_M2 + tt:S_M2 + tt + 1]
                    g2 = sml[:, tt * 8:tt * 8 + 8]
                    mk1 = sml[:, 64 + tt * 8:64 + tt * 8 + 8]
                    v.tensor_reduce(m1, g8, AX.X, ALU.max)
                    v.drain()
                    v.tensor_scalar(mk1, g8, m1, None, ALU.is_ge)
                    v.drain()
                    v.scalar_tensor_tensor(g2, mk1, -1e30, g8, ALU.mult, ALU.add)
                    v.drain()
                    v.tensor_reduce(m2, g2, AX.X, ALU.max)
                    v.drain()
                    ins = v.tensor_tensor(m2, m2, m1, ALU.subtract)
                    v.drain()
                em.inc(ins, "EVV", 1)
                em.mark("EVV", ("m12", k))

            block.vector(v_p1)

            def s_p2(s, k=k):
                em.wait(s, "EVV", ("m12", k))
                ins = s.activation(state[:, S_M2:S_M2 + 4], state[:, S_M2:S_M2 + 4],
                                   AF.Exp)
                em.inc(ins, "EVA", 1)
                em.mark("EVA", ("exp", k))

            block.scalar(s_p2)

            def v_p2(v, k=k):
                em.wait(v, "EVA", ("exp", k))
                for tt in range(4):
                    g8 = glog[:, tt * 8:(tt + 1) * 8]
                    g2 = sml[:, tt * 8:tt * 8 + 8]
                    mk1 = sml[:, 64 + tt * 8:64 + tt * 8 + 8]
                    mk2 = sml[:, 128 + tt * 8:128 + tt * 8 + 8]
                    ee = state[:, S_M2 + tt:S_M2 + tt + 1]
                    w1 = state[:, S_T1:S_T1 + 1]
                    w2 = state[:, S_T2:S_T2 + 1]
                    m2v = state[:, S_T3:S_T3 + 1]
                    v.tensor_single_scalar(w1, ee, 1.0, ALU.add)
                    v.drain()
                    v.reciprocal(w1, w1)
                    v.drain()
                    v.tensor_single_scalar(w1, w1, SCALING, ALU.mult)
                    v.drain()
                    v.tensor_single_scalar(w2, w1, -1.0, ALU.mult)
                    v.drain()
                    v.tensor_single_scalar(w2, w2, SCALING, ALU.add)
                    v.tensor_reduce(m2v, g2, AX.X, ALU.max)
                    v.drain()
                    v.tensor_scalar(mk2, g2, m2v, None, ALU.is_ge)
                    v.drain()
                    v.tensor_scalar(mk1, mk1, w1, None, ALU.mult)
                    v.tensor_scalar(mk2, mk2, w2, None, ALU.mult)
                    v.drain()
                    ins = v.tensor_tensor(wful[:, tt * 8:(tt + 1) * 8], mk1, mk2,
                                          ALU.add)
                    v.drain()
                em.inc(ins, "EVV", 1)
                em.mark("EVV", ("wful", k))

            block.vector(v_p2)

            def t_p3(t, k=k):
                em.wait(t, "EVV", ("wful", k))
                for tt in range(4):
                    ins = t.transpose(pk[2][0:8, tt * 128:(tt + 1) * 128],
                                      wful[:, tt * 8:(tt + 1) * 8], ident32[:])
                em.inc(ins, "MM", 1)
                em.mark("MM", ("mmwft", k))

            block.tensor(t_p3)

            def s_p3(s, k=k):
                em.wait(s, "MM", ("mmwft", k))
                ins = s.copy(wfT[:], pk[2][0:8, 0:512])
                em.inc(ins, "EVA", 1)
                em.mark("EVA", ("wftev", k))

            block.scalar(s_p3)

            def t_p4(t, k=k):
                em.wait(t, "EVA", ("wftev", k))
                ins = t.matmul(pk[3][:, 0:512], expB[:], wfT[:],
                               start=True, stop=True)
                em.inc(ins, "MM", 1)
                em.mark("MM", ("mmwex", k))

            block.tensor(t_p4)

            def v_p3(v, k=k):
                em.wait(v, "MM", ("mmlow", k))
                v.tensor_copy(lowS[:], pk[4][:, 0:512])
                v.drain()
                em.wait(v, "MM", ("mmwex", k))
                ins = v.tensor_tensor(lsT[:], lowS[:], pk[3][:, 0:512], ALU.mult)
                em.inc(ins, "EVV", 1)
                em.mark("EVV", ("lsT", k))

            block.vector(v_p3)

            if taps:
                def sy_dbg4(g, k=k):
                    em.wait(g, "EVV", ("lsT", k))
                    ins = g.dma_start(out=dbg4[k, :, 0:512], in_=lsT[:])
                    em.inc(ins, "STG", 16)
                    ins = g.dma_start(out=dbg4[k, :, 512:544], in_=glog[:])
                    em.inc(ins, "STG", 16)
                    ins = g.dma_start(out=dbg4[k, :, 544:576], in_=wful[:])
                    em.inc(ins, "STG", 16)
                    ins = g.dma_start(out=dbg4[k, :, 576:1088], in_=lowS[:])
                    em.inc(ins, "STG", 16)
                    ins = g.dma_start(out=dbg4[k, :, 1088:1120],
                                      in_=hTb[:, 0:32])
                    em.inc(ins, "STG", 16)
                    ins = g.dma_start(out=dbg4[k, :, 1120:1152],
                                      in_=laB[:, 2048:2080])
                    em.inc(ins, "STG", 16)

                block.gpsimd(sy_dbg4)

            def sy_4(sy, k=k):
                if os.environ.get("S4SER", "0") == "1":
                    em.wait(sy, "EVV", ("lsT", k))
                if k == 0:
                    em.wait(sy, "MM", ("mmoct", NSLAB - 1, 7, NOCT - 1))

                def wbload(oc):
                    if oc >= 2:
                        em.wait(sy, "MM", ("mm4", k, oc - 2, 3))
                    elif k >= 1:
                        em.wait(sy, "MM", ("mm4", k - 1, 6 + oc, 3))
                    src = wb2[oc]
                    dst = wbB[:, (oc % 2) * NJ * 512:
                              ((oc % 2) + 1) * NJ * 512].rearrange(
                                  "p (j w) -> p j w", j=NJ)
                    ins = sy.dma_start(out=dst, in_=src)
                    em.inc(ins, "ST", 16)
                    ins = sy.dma_start(
                        out=biasOC[0:1, (oc % 2) * 512:(oc % 2 + 1) * 512],
                        in_=bias_f16[0:1, oc * 512:(oc + 1) * 512])
                    em.inc(ins, "ST", 16)
                    em.mark("ST", ("wb", k, oc))

                wbload(0)
                for oc in range(8):
                    if oc + 1 < 8:
                        wbload(oc + 1)
                    if oc >= 1:
                        for tt in range(4):
                            em.wait(sy, "EVV", ("ev4", k, oc - 1, tt))
                            pidx = (oc - 1) * 4 + tt
                            ins = sy.dma_start(
                                out=y_scr[r0 + tt * 128:r0 + tt * 128 + 128,
                                          (oc - 1) * 512:oc * 512],
                                in_=yCh[:, (pidx % 2) * 512:(pidx % 2 + 1) * 512])
                            em.inc(ins, "ST", 16)
                            em.mark("ST", ("yst", k, oc - 1, tt))
                    for tt in range(4):
                        idx = oc * 4 + tt
                        if idx >= 2:
                            em.wait(sy, "EVV", ("ev4", k, (idx - 2) // 4,
                                                (idx - 2) % 4))
                        elif k == 0:
                            em.wait(sy, "EVV", ("ev2", NSLAB - 1, 7, 3))
                        else:
                            em.wait(sy, "EVV", ("ev4", k - 1, 7, 2 + idx))
                        em.wait(sy, "HS", ("hs", k, tt * 8 + oc))
                        ins = sy.dma_start(
                            out=hRes[:, (idx % 2) * 512:(idx % 2 + 1) * 512],
                            in_=h_scr[r0 + tt * 128:r0 + tt * 128 + 128,
                                      oc * 512:(oc + 1) * 512])
                        em.inc(ins, "ST", 16)
                        em.mark("ST", ("h4", k, oc, tt))
                for tt in range(4):
                    em.wait(sy, "EVV", ("ev4", k, 7, tt))
                    pidx = 7 * 4 + tt
                    ins = sy.dma_start(
                        out=y_scr[r0 + tt * 128:r0 + tt * 128 + 128,
                                  7 * 512:8 * 512],
                        in_=yCh[:, (pidx % 2) * 512:(pidx % 2 + 1) * 512])
                    em.inc(ins, "ST", 16)
                    em.mark("ST", ("yst", k, 7, tt))
                for tt in range(4):
                    for ds in range(8):
                        idx = tt * 8 + ds
                        em.wait(sy, "ST", ("yst", k, ds, tt))
                        if idx >= 2:
                            em.wait(sy, "EVA", ("app4", k, idx - 2))
                        elif k == 0:
                            em.wait(sy, "EVA", ("app", NSLAB - 1, 31))
                        else:
                            em.wait(sy, "EVA", ("app4", k - 1, 30 + idx))
                        ins = sy.dma_start(
                            out=ylCh[:, (idx % 2) * 512:(idx % 2 + 1) * 512],
                            in_=y_scr[r0 + tt * 128:r0 + tt * 128 + 128,
                                      ds * 512:(ds + 1) * 512])
                        em.inc(ins, "ST", 16)
                        em.mark("ST", ("yl", k, idx))
                        if idx >= 1:
                            pidx = idx - 1
                            ptt, pds = pidx // 8, pidx % 8
                            em.wait(sy, "EVA", ("app4", k, pidx))
                            ins = sy.dma_start(
                                out=y_out[r0 + ptt * 128:r0 + ptt * 128 + 128,
                                          pds * 512:(pds + 1) * 512],
                                in_=yoCh[:, (pidx % 2) * 512:(pidx % 2 + 1) * 512])
                            em.inc(ins, "ST", 16)
                            em.mark("ST", ("yo", k, pidx))
                em.wait(sy, "EVA", ("app4", k, 31))
                ins = sy.dma_start(
                    out=y_out[r0 + 3 * 128:r0 + 4 * 128, 7 * 512:8 * 512],
                    in_=yoCh[:, (31 % 2) * 512:(31 % 2 + 1) * 512])
                em.inc(ins, "ST", 16)
                em.mark("ST", ("yo", k, 31))

            block.sync(sy_4)

            def t_4(t, k=k):
                em.wait(t, "EVV", ("lsT", k))
                for oc in range(8):
                    bset = pk[0:4] if oc % 2 == 0 else pk[4:8]
                    em.wait(t, "ST", ("wb", k, min(oc + 1, 7)))
                    if oc >= 2:
                        em.wait(t, "EVV", ("ev4", k, oc - 2, 3))
                    wbv = wbB[:, (oc % 2) * NJ * 512:((oc % 2) + 1) * NJ * 512]
                    for tt in range(4):
                        for j in range(NJ):
                            t.matmul(bset[tt][:, 0:512],
                                     hTb[:, j * 512 + tt * 128:
                                         j * 512 + (tt + 1) * 128],
                                     wbv[:, j * 512:(j + 1) * 512],
                                     start=(j == 0), stop=False)
                        t.matmul(bset[tt][:, 0:512], onesB[:],
                                 biasOC[:, (oc % 2) * 512:(oc % 2 + 1) * 512],
                                 start=False, stop=False)
                        ins = t.matmul(bset[tt][:, 0:512],
                                       lsT[:, tt * 128:(tt + 1) * 128],
                                       lbB[:, oc * 512:(oc + 1) * 512],
                                       start=False, stop=True)
                        em.inc(ins, "MM", 1)
                        em.mark("MM", ("mm4", k, oc, tt))
                em.mark("MM", ("mm4fin", k))

            block.tensor(t_4)

            def v_4(v, k=k):
                for oc in range(8):
                    bset = pk[0:4] if oc % 2 == 0 else pk[4:8]
                    for tt in range(4):
                        idx = oc * 4 + tt
                        em.wait(v, "MM", ("mm4", k, oc, tt))
                        em.wait(v, "ST", ("h4", k, oc, min(tt + 1, 3)))
                        if idx >= 2:
                            em.wait(v, "ST", ("yst", k, (idx - 2) // 4,
                                              (idx - 2) % 4))
                        elif k == 0:
                            em.wait(v, "ST", ("rst", NSLAB - 1, 7, 3))
                        else:
                            em.wait(v, "ST", ("yst", k - 1, 7, 2 + idx))
                        cp = idx % 2
                        v.scalar_tensor_tensor(
                            yCh[:, cp * 512:(cp + 1) * 512],
                            bset[tt][:, 0:512], 1.0,
                            hRes[:, cp * 512:(cp + 1) * 512],
                            ALU.mult, ALU.add,
                            accum_out=state[:, S_SUM + tt * 8 + oc:
                                            S_SUM + tt * 8 + oc + 1])
                        ins = v.scalar_tensor_tensor(
                            sqJ4[:], yCh[:, cp * 512:(cp + 1) * 512], 1.0,
                            yCh[:, cp * 512:(cp + 1) * 512],
                            ALU.mult, ALU.mult,
                            accum_out=state[:, S_SQ + tt * 8 + oc:
                                            S_SQ + tt * 8 + oc + 1])
                        em.inc(ins, "EVV", 1)
                        em.mark("EVV", ("ev4", k, oc, tt))
                if k >= 2:
                    em.wait(v, "EVA", ("app4", k - 2, 31))
                v.drain()
                MUB = S_MU2_P[k % 2]
                RSB = S_RS2_P[k % 2]
                for tt in range(4):
                    mu = state[:, MUB + tt:MUB + tt + 1]
                    va = state[:, S_T1:S_T1 + 1]
                    v.tensor_reduce(mu, state[:, S_SUM + tt * 8:S_SUM + tt * 8 + 8],
                                    AX.X, ALU.add)
                    v.tensor_reduce(va, state[:, S_SQ + tt * 8:S_SQ + tt * 8 + 8],
                                    AX.X, ALU.add)
                    v.drain()
                    v.tensor_single_scalar(mu, mu, 1.0 / D, ALU.mult)
                    v.tensor_single_scalar(va, va, 1.0 / D, ALU.mult)
                    v.drain()
                    v.tensor_tensor(state[:, S_T2:S_T2 + 1], mu, mu, ALU.mult)
                    v.drain()
                    v.tensor_tensor(va, va, state[:, S_T2:S_T2 + 1], ALU.subtract)
                    v.drain()
                    ins = v.tensor_single_scalar(
                        state[:, RSB + tt:RSB + tt + 1], va, LN_EPS, ALU.add)
                    em.inc(ins, "EVV", 1)
                    em.mark("EVV", ("var4", k, tt))

            block.vector(v_4)

            def s_ln2(s, k=k):
                RSB = S_RS2_P[k % 2]
                for tt in range(4):
                    em.wait(s, "EVV", ("var4", k, tt))
                    ins = s.activation(state[:, RSB + tt:RSB + tt + 1],
                                       state[:, RSB + tt:RSB + tt + 1], AF.Sqrt)
                    em.inc(ins, "EVA", 1)
                    em.mark("EVA", ("sqrt4", k, tt))

            block.scalar(s_ln2)

            def v_fin2(v, k=k):
                MUB = S_MU2_P[k % 2]
                RSB = S_RS2_P[k % 2]
                BIB = S_BI2_P[k % 2]
                for tt in range(4):
                    em.wait(v, "EVA", ("sqrt4", k, tt))
                    rs = state[:, RSB + tt:RSB + tt + 1]
                    v.reciprocal(rs, rs)
                    v.drain()
                    ins = v.scalar_tensor_tensor(
                        state[:, BIB + tt:BIB + tt + 1],
                        state[:, MUB + tt:MUB + tt + 1],
                        -1.0, rs, ALU.mult, ALU.mult)
                    em.inc(ins, "EVV", 1)
                    em.mark("EVV", ("fin4", k, tt))

            block.vector(v_fin2)

            def s_apply2(s, k=k):
                RSB = S_RS2_P[k % 2]
                BIB = S_BI2_P[k % 2]
                for tt in range(4):
                    em.wait(s, "EVV", ("fin4", k, tt))
                    for ds in range(8):
                        idx = tt * 8 + ds
                        em.wait(s, "ST", ("yl", k, idx))
                        if idx >= 2:
                            em.wait(s, "ST", ("yo", k, idx - 2))
                        elif k == 0:
                            em.wait(s, "HS", ("hs", NSLAB - 1, 31))
                            em.wait(s, "STG", ("hc", NSLAB - 1, 31))
                        else:
                            em.wait(s, "ST", ("yo", k - 1, 30 + idx))
                        ins = s.activation(
                            yoCh[:, (idx % 2) * 512:(idx % 2 + 1) * 512],
                            ylCh[:, (idx % 2) * 512:(idx % 2 + 1) * 512],
                            AF.Identity,
                            bias=state[:, BIB + tt:BIB + tt + 1],
                            scale=state[:, RSB + tt:RSB + tt + 1])
                        em.inc(ins, "EVA", 1)
                        em.mark("EVA", ("app4", k, idx))

            block.scalar(s_apply2)

        def sy_fin(sy):
            em.waitv(sy, "RL", em.n["RL"])
            em.waitv(sy, "HS", em.n["HS"])
            em.waitv(sy, "SEL", em.n["SEL"])
            em.waitv(sy, "ST", em.n["ST"])
            em.waitv(sy, "STG", em.n["STG"])
            em.waitv(sy, "LDT", em.n["LDT"])
            em.waitv(sy, "LDW", em.n["LDW"])

        block.sync(sy_fin)

    return nc


def build_program(n_tok=2048, taps=False, stages=4):
    rec = {}
    _emit(n_tok, taps, rec, final=False, stages=stages)
    return _emit(n_tok, taps, rec, final=True, stages=stages)


def _pack_gw(gwT):
    out = np.zeros((128, NJ * 128), dtype=np.float32)
    g = gwT.reshape(NJ, 128, 8)
    for j in range(NJ):
        out[:, j * 128:j * 128 + 8] = g[j]
    return out


def kernel(L0_attn_x, L0_attn_w_gate, L0_attn_w_up, L0_attn_w_down,
           L0_ffn_gate_w, L0_ffn_base_w, L0_ffn_base_b,
           L0_ffn_lora_A, L0_ffn_lora_B):
    n_cores = 8
    B, S, _ = L0_attn_x.shape
    ntok = B * S
    per = ntok // n_cores
    x2 = np.ascontiguousarray(np.asarray(L0_attn_x, dtype=np.float32).reshape(ntok, D))

    def T(a):
        return np.ascontiguousarray(np.asarray(a, dtype=np.float32).T)

    consts = {
        "wgT": T(L0_attn_w_gate), "wuT": T(L0_attn_w_up), "wdT": T(L0_attn_w_down),
        "wbT": T(L0_ffn_base_w),
        "gwP": _pack_gw(T(L0_ffn_gate_w)),
        "laT": T(np.asarray(L0_ffn_lora_A, dtype=np.float32).reshape(128, D)),
        "lbT": np.ascontiguousarray(
            np.transpose(np.asarray(L0_ffn_lora_B, dtype=np.float32),
                         (0, 2, 1)).reshape(128, D)),
        "bias": np.asarray(L0_ffn_base_b, dtype=np.float32).reshape(1, D),
        "ident": np.eye(128, dtype=np.float32),
        "expm": np.repeat(np.eye(8, dtype=np.float32), 16, axis=1).reshape(8, 128),
        "ones": np.concatenate([np.ones((1, 128), np.float32),
                                np.zeros((127, 128), np.float32)]),
    }
    nc = build_program(n_tok=per, taps=False)
    in_maps = []
    for c in range(n_cores):
        xs = x2[c * per:(c + 1) * per]
        m = {"x": xs, "xT": np.ascontiguousarray(xs.T)}
        m.update(consts)
        in_maps.append(m)
    res = run_bass_kernel_spmd(nc, in_maps, core_ids=list(range(n_cores)))
    y = np.concatenate([np.asarray(r["y"], dtype=np.float32) for r in res.results],
                       axis=0)
    return y.reshape(B, S, D)
